# version 01
# baseline (speedup 1.0000x reference)
import numpy as np
import ml_dtypes
import concourse.bass as bass
import concourse.mybir as mybir
from concourse.bass_utils import run_bass_kernel_spmd
from concourse.bass import IndirectOffsetOnAxis
from contextlib import ExitStack

F32 = mybir.dt.float32
BF16 = mybir.dt.bfloat16
I32 = mybir.dt.int32
AF = mybir.ActivationFunctionType
ALU = mybir.AluOpType
AXX = mybir.AxisListType.X

import os
_TRIM = int(os.environ.get('KTRIM', '0'))

T = 4096
D = 1024
NCH = D // 128
NT = T // 512
NFOLD = T // 128
LN_EPS = 1e-5
BF = ml_dtypes.bfloat16

XN0 = 0
SQ0 = 32768
WRK = 36864
ABF_COLS = 68608
TMP0 = 0
TMP1 = 4096
MISC = 8192
AF32_COLS = 8704


NPOOL = 12


class Eng:

    def __init__(self, ctx, key, obj, step):
        self.ctx, self.key, self.obj, self.step = ctx, key, obj, step
        if step == 16:
            self.pool = []
            for i in range(NPOOL):
                nm = f"{key}{i}"
                h = ctx.es.enter_context(ctx.nc.semaphore(f"S{nm}"))
                ctx.sems[nm] = h
                self.pool.append(nm)
            self.cnts = [0] * NPOOL
            self.i = 0
        else:
            self.sem = ctx.es.enter_context(ctx.nc.semaphore(f"S{key}"))
            ctx.sems[key] = self.sem
            self.cnt = 0
        self.waited = {}

    def wait(self, *tokens):
        for tok in tokens:
            if tok is None:
                continue
            if isinstance(tok, (list, tuple)) and (
                    not tok or isinstance(tok[0], (list, tuple))):
                self.wait(*tok)
                continue
            k, v = tok
            if k is None or v is None or v <= 0:
                continue
            if self.waited.get(k, 0) < v:
                self.obj.wait_ge(self.ctx.sems[k], v)
                self.waited[k] = v

    def run(self, fn, *tokens):
        self.wait(*tokens)
        if self.step == 16:
            i = self.i % NPOOL
            nm = self.pool[i]
            if self.cnts[i] > 0:
                self.wait((nm, self.cnts[i]))
            inst = fn(self.obj)
            inst.then_inc(self.ctx.sems[nm], 16)
            self.cnts[i] += 16
            self.i += 1
            return (nm, self.cnts[i])
        inst = fn(self.obj)
        inst.then_inc(self.sem, self.step)
        self.cnt += self.step
        return (self.key, self.cnt)

    def tok(self):
        if self.step == 16:
            return [(nm, c) for nm, c in zip(self.pool, self.cnts) if c > 0]
        return (self.key, self.cnt)


class Bld:

    def __init__(self, nc, es):
        self.nc, self.es = nc, es
        self.eng = {}
        self.sems = {}
        self.c = {}

    def init(self, io):
        nc, es = self.nc, self.es
        self.io = io
        self.eng = {
            't': Eng(self, 't', nc.tensor, 1),
            'v': Eng(self, 'v', nc.vector, 1),
            's': Eng(self, 's', nc.scalar, 1),
            'g': Eng(self, 'g', nc.gpsimd, 1),
            'q': Eng(self, 'q', nc.sync, 16),
            'gd': Eng(self, 'gd', nc.gpsimd, 16),
        }
        for e in self.eng.values():
            e.ctx = self
        self.ABF = es.enter_context(nc.sbuf_tensor("ABF", [128, ABF_COLS], BF16))
        self.AF32 = es.enter_context(nc.sbuf_tensor("AF32", [128, AF32_COLS], F32))
        self.AI32 = es.enter_context(nc.sbuf_tensor("AI32", [128, 64], I32))
        self.pb = [es.enter_context(nc.psum_tensor(f"pb{i}", [128, 512], F32))
                   for i in range(8)]
        self.ident = es.enter_context(nc.sbuf_tensor("ident", [128, 128], F32))
        self.identb = es.enter_context(nc.sbuf_tensor("identb", [128, 128], BF16))
        self.onescol = es.enter_context(nc.sbuf_tensor("onescol", [128, 2], BF16))
        self.onesrow = es.enter_context(nc.sbuf_tensor("onesrow", [1, 128], F32))
        self.row32 = es.enter_context(nc.sbuf_tensor("row32", [32, 512], F32))
        self.rowTa = es.enter_context(nc.sbuf_tensor("rowTa", [1, T], F32))
        self.rowTb = es.enter_context(nc.sbuf_tensor("rowTb", [1, T], F32))
        self.constf = es.enter_context(nc.sbuf_tensor("constf", [128, 8], F32))
        self.AI32b = es.enter_context(nc.sbuf_tensor("AI32b", [128, 1024], I32))
        self.bounce = nc.dram_tensor("bounce", [32, 128], F32, kind="Internal").ap()
        self.bounce2 = nc.dram_tensor("bounce2", [32, 128], F32, kind="Internal").ap()

    @property
    def t(self): return self.eng['t']
    @property
    def v(self): return self.eng['v']
    @property
    def s(self): return self.eng['s']
    @property
    def g(self): return self.eng['g']
    @property
    def q(self): return self.eng['q']
    @property
    def gd(self): return self.eng['gd']

    def xn(self, c, j0=0, n=T):
        return self.ABF[:, XN0 + c*T + j0: XN0 + c*T + j0 + n]

    def load_consts(self):
        q, gd = self.q, self.gd
        ci = self.io['consts_in']
        t1 = q.run(lambda e: e.dma_start(out=self.ident[:, :], in_=ci[:, 0:128]))
        t2 = self.gd.run(lambda e: e.dma_start(out=self.identb[:, :], in_=ci[:, 0:128]))
        t3 = self.gd.run(lambda e: e.dma_start(out=self.onescol[:, :], in_=ci[:, 128:130]))
        t4 = q.run(lambda e: e.dma_start(out=self.onesrow[0:1, :], in_=ci[0:1, 130:258]))
        t5 = q.run(lambda e: e.dma_start(out=self.constf[:, :], in_=ci[:, 258:266]))
        return [t1, t2, t3, t4, t5]


def consts_array():
    a = np.zeros((128, 266), np.float32)
    a[:, 0:128] = np.eye(128, dtype=np.float32)
    a[:, 128:130] = 1.0
    a[0, 130:258] = 1.0
    a[:, 258] = LN_EPS
    a[:, 259] = 1.0
    a[:, 260] = -1e30
    a[:, 261] = 1.0/7.5
    return a



def emit_ln_xn(bld, src_dram, after=()):
    t, v, s, q = bld.t, bld.v, bld.s, bld.q
    AB, AF = bld.ABF, bld.AF32
    pb = bld.pb
    PS_SUM, PS_SQ, PS_TR, PS_A, PS_B = pb[6], pb[7], pb[5], pb[4], pb[5]

    bufs = [AF[:, TMP0:TMP0+T], AF[:, TMP1:TMP1+T]]
    prev = [None, None]
    mm_toks = []
    for c in range(NCH):
        b = c % 2
        ld = q.run(lambda e, c=c, b=b: e.dma_start(
            out=bufs[b][:, 0:T], in_=src_dram[c*128:(c+1)*128, :]),
            prev[b], *after)
        cast = v.run(lambda e, c=c, b=b: e.tensor_copy(bld.xn(c), bufs[b][:, 0:T]), ld)
        sq = s.run(lambda e, c=c: e.activation(
            out=AB[:, SQ0:SQ0+T], in_=bld.xn(c), func=AF_.Square), cast,
            mm_toks[-1] if c > 0 else None)
        prev[b] = cast
        for j in range(NFOLD):
            t.run(lambda e, c=c, j=j: e.matmul(
                PS_SUM[:, c*32 + j: c*32 + j + 1],
                bld.xn(c, j*128, 128), bld.onescol[:, 0:1],
                start=True, stop=True), cast)
            mm_toks.append(t.run(lambda e, c=c, j=j: e.matmul(
                PS_SQ[:, c*32 + j: c*32 + j + 1],
                AB[:, SQ0 + j*128: SQ0 + (j+1)*128], bld.onescol[:, 0:1],
                start=True, stop=True), sq))
    last_mm = mm_toks[-1]

    M = AF[:, MISC:MISC+512]
    tk = v.run(lambda e: e.tensor_copy(M[:, 0:32], PS_SUM[:, 0:32]), last_mm)
    for c in range(1, NCH):
        tk = v.run(lambda e, c=c: e.tensor_tensor(
            M[:, 0:32], M[:, 0:32], PS_SUM[:, c*32:(c+1)*32], ALU.add), tk)
    tk2 = v.run(lambda e: e.tensor_copy(M[:, 32:64], PS_SQ[:, 0:32]), last_mm)
    for c in range(1, NCH):
        tk2 = v.run(lambda e, c=c: e.tensor_tensor(
            M[:, 32:64], M[:, 32:64], PS_SQ[:, c*32:(c+1)*32], ALU.add), tk2)
    tk = v.run(lambda e: e.tensor_scalar(M[:, 0:32], M[:, 0:32], 1.0/D, None,
                                         ALU.mult), tk)
    tk3 = v.run(lambda e: e.tensor_tensor(M[:, 128:160], M[:, 0:32], M[:, 0:32],
                                          ALU.mult), tk)
    tk3 = v.run(lambda e: e.scalar_tensor_tensor(
        M[:, 32:64], M[:, 32:64], 1.0/D, M[:, 128:160], ALU.mult, ALU.subtract),
        tk3, tk2)
    tk3 = s.run(lambda e: e.activation(out=M[:, 128:160], in_=M[:, 32:64],
                                       func=AF_.Sqrt, bias=bld.constf[:, 0:1]), tk3)
    tka = v.run(lambda e: e.reciprocal(M[:, 64:96], M[:, 128:160]), tk3)
    tkb = v.run(lambda e: e.tensor_tensor(M[:, 96:128], M[:, 0:32], M[:, 64:96],
                                          ALU.mult), tka)
    tkb = v.run(lambda e: e.tensor_scalar(M[:, 96:128], M[:, 96:128], -1.0, None,
                                          ALU.mult), tkb)

    tra = t.run(lambda e: e.matmul(PS_TR[0:32, 0:128], M[:, 64:96],
                                   bld.ident[:, :], start=True, stop=True), tka)
    trb = t.run(lambda e: e.matmul(PS_TR[0:32, 128:256], M[:, 96:128],
                                   bld.ident[:, :], start=True, stop=True), tkb)
    ev = v.run(lambda e: e.tensor_copy(bld.row32[0:32, 0:256], PS_TR[0:32, 0:256]),
               tra, trb)
    st1 = q.run(lambda e: e.dma_start(out=bld.bounce[0:32, 0:128],
                                      in_=bld.row32[0:32, 0:128]), ev)
    st2 = q.run(lambda e: e.dma_start(out=bld.bounce2[0:32, 0:128],
                                      in_=bld.row32[0:32, 128:256]), ev)
    lda = q.run(lambda e: e.dma_start(
        out=bld.rowTa[0:1, 0:T],
        in_=bld.bounce.rearrange("a b -> () (a b)")), st1)
    ldb = q.run(lambda e: e.dma_start(
        out=bld.rowTb[0:1, 0:T],
        in_=bld.bounce2.rearrange("a b -> () (a b)")), st2)

    toks = []
    lastm2 = None
    for j in range(NT):
        bca = t.run(lambda e, j=j: e.matmul(
            PS_A[:, 0:512], bld.onesrow[0:1, :], bld.rowTa[0:1, j*512:(j+1)*512],
            start=True, stop=True), lda, lastm2)
        bcb = t.run(lambda e, j=j: e.matmul(
            PS_B[:, 0:512], bld.onesrow[0:1, :], bld.rowTb[0:1, j*512:(j+1)*512],
            start=True, stop=True), ldb, ev)
        for c in range(NCH):
            k = (j*NCH + c) % 2
            m1 = v.run(lambda e, c=c, j=j, k=k: e.tensor_tensor(
                AF[:, TMP0+k*512:TMP0+k*512+512], bld.xn(c, j*512, 512),
                PS_A[:, 0:512], ALU.mult), bca,
                prev[0] if c == 0 and j == 0 else None,
                prev[1] if c == 0 and j == 0 else None,
                toks[-2] if len(toks) >= 2 else None)
            m2 = v.run(lambda e, c=c, j=j, k=k: e.tensor_tensor(
                bld.xn(c, j*512, 512), AF[:, TMP0+k*512:TMP0+k*512+512],
                PS_B[:, 0:512], ALU.add), m1, bcb)
            toks.append(m2)
        lastm2 = toks[-1]
    return toks


AF_ = AF


def fold_to_row(bld, fold_view, row_view, bounce, ps, after=(), war=(), k=32):
    t, v, q = bld.t, bld.v, bld.q
    tr = t.run(lambda e: e.matmul(ps[0:k, 0:128], fold_view, bld.ident[:, :],
                                  start=True, stop=True), *after)
    ev = v.run(lambda e: e.tensor_copy(bld.row32[0:k, 0:128], ps[0:k, 0:128]), tr)
    st = q.run(lambda e: e.dma_start(out=bounce[0:k, 0:128],
                                     in_=bld.row32[0:k, 0:128]), ev, *war)
    ld = q.run(lambda e: e.dma_start(
        out=row_view,
        in_=bounce[0:k, 0:128].rearrange("a b -> () (a b)")), st, *war)
    return ld



def emit_l1_attn(bld, src_dram, dst_dram, after=()):
    t, v, s, q, gd = bld.t, bld.v, bld.s, bld.q, bld.gd
    AB, AF32_ = bld.ABF, bld.AF32
    io = bld.io
    M = AF32_[:, MISC:MISC+512]
    WaT = AB[:, WRK:WRK+256]
    WinT = AB[:, WRK+256:WRK+512]
    WoutT = AB[0:32, WRK+512:WRK+1536]
    betab = AB[:, WRK+1536:WRK+1544]
    BIAS = AF32_[:, MISC+160:MISC+192]
    lw = []
    for c in range(NCH):
        lw.append(gd.run(lambda e, c=c: e.dma_start(
            out=WaT[:, c*32:(c+1)*32], in_=io['l1a_WaT'][c*128:(c+1)*128, :])))
        lw.append(gd.run(lambda e, c=c: e.dma_start(
            out=WinT[:, c*32:(c+1)*32], in_=io['l1a_WinT'][c*128:(c+1)*128, :])))
    lw.append(gd.run(lambda e: e.dma_start(out=WoutT[0:32, :], in_=io['l1a_WoutT'][:, :])))
    ldb = bld.q.run(lambda e: e.dma_start(out=BIAS[:, 0:26], in_=io['l1a_bias'][:, :]))
    lw.append(ldb)
    cb = v.run(lambda e: e.tensor_copy(betab[:, 0:8], BIAS[:, 18:26]), ldb, *after)

    psWB = bld.pb[3]
    mmb = None
    for c in range(NCH):
        mmb = t.run(lambda e, c=c: e.matmul(
            psWB[0:32, 0:1], WaT[:, c*32:(c+1)*32], betab[:, c:c+1],
            start=(c == 0), stop=False), cb, lw)
        mmb = t.run(lambda e, c=c: e.matmul(
            psWB[0:32, 2:3], WinT[:, c*32:(c+1)*32], betab[:, c:c+1],
            start=False, stop=(c == NCH-1)))
    bb = v.run(lambda e: e.tensor_tensor(BIAS[0:32, 26:27], psWB[0:32, 0:1],
                                         BIAS[0:32, 0:1], ALU.add), mmb)
    bb = v.run(lambda e: e.tensor_tensor(BIAS[0:32, 27:28], psWB[0:32, 2:3],
                                         BIAS[0:32, 1:2], ALU.add), bb)

    wg = [bb]
    for c in range(NCH):
        wg.append(v.run(lambda e, c=c: e.tensor_scalar(
            WaT[:, c*32:(c+1)*32], WaT[:, c*32:(c+1)*32], BIAS[:, 10+c:11+c],
            None, ALU.mult), bb))
        wg.append(v.run(lambda e, c=c: e.tensor_scalar(
            WinT[:, c*32:(c+1)*32], WinT[:, c*32:(c+1)*32], BIAS[:, 10+c:11+c],
            None, ALU.mult)))

    c0 = bld.constf
    a2t = v.run(lambda e: e.tensor_scalar(M[:, 224:256], M[:, 32:64], LN_EPS,
                                          None, ALU.add), *after)
    a2t = v.run(lambda e: e.reciprocal(M[:, 256:288], M[:, 224:256]), a2t)
    a2t = v.run(lambda e: e.tensor_tensor(M[:, 256:288], M[:, 32:64],
                                          M[:, 256:288], ALU.mult), a2t)
    a2t = s.run(lambda e: e.activation(out=M[:, 256:288], in_=M[:, 256:288],
                                       func=AF_.Sqrt, bias=c0[:, 0:1]), a2t)
    a2t = v.run(lambda e: e.reciprocal(M[:, 256:288], M[:, 256:288]), a2t)
    a2row = fold_to_row(bld, M[:, 256:288], bld.rowTb[0:1, 0:T], bld.bounce2,
                        bld.pb[3], after=[a2t], war=list(after))

    psA = bld.pb[0]
    psB = bld.pb[1]
    psA2 = bld.pb[2]
    ps_out = bld.pb[2]
    AL = AF32_[0:32, TMP0:TMP0+T]
    BE = AF32_[0:32, TMP1:TMP1+T]
    A2S = AF32_[64:96, TMP1:TMP1+T]
    Hb = AB[0:32, SQ0:SQ0+T]
    acts = []
    prev_a = None
    for j in range(NT):
        mm = None
        for c in range(NCH):
            mm = t.run(lambda e, c=c, j=j: e.matmul(
                psA[0:32, 0:512], WaT[:, c*32:(c+1)*32], bld.xn(c, j*512, 512),
                start=(c == 0), stop=(c == NCH-1)), wg, list(after),
                *( [prev_a] if prev_a else []))
            mm = t.run(lambda e, c=c, j=j: e.matmul(
                psB[0:32, 0:512], WinT[:, c*32:(c+1)*32], bld.xn(c, j*512, 512),
                start=(c == 0), stop=(c == NCH-1)))
        bc = t.run(lambda e, j=j: e.matmul(
            psA2[0:32, 0:512], bld.onesrow[0:1, 0:32],
            bld.rowTb[0:1, j*512:(j+1)*512], start=True, stop=True), a2row,
            *( [prev_a] if prev_a else []))
        a2s = v.run(lambda e, j=j: e.tensor_copy(
            A2S[:, j*512:(j+1)*512], psA2[0:32, 0:512]), bc, mm, list(after))
        st1 = v.run(lambda e, j=j: e.tensor_tensor(
            AL[:, j*512:(j+1)*512], psA[0:32, 0:512],
            A2S[:, j*512:(j+1)*512], ALU.mult), a2s)
        st2 = v.run(lambda e, j=j: e.tensor_tensor(
            BE[:, j*512:(j+1)*512], psB[0:32, 0:512],
            A2S[:, j*512:(j+1)*512], ALU.mult), st1)
        a1 = s.run(lambda e, j=j: e.activation(
            out=AL[:, j*512:(j+1)*512], in_=AL[:, j*512:(j+1)*512],
            func=AF_.Sigmoid, bias=BIAS[0:32, 26:27]), st1)
        a2_ = s.run(lambda e, j=j: e.activation(
            out=BE[:, j*512:(j+1)*512], in_=BE[:, j*512:(j+1)*512],
            func=AF_.Identity, bias=BIAS[0:32, 27:28]), st2)
        prev_a = a2_
        acts += [a1, a2_]
    if _TRIM == 2:
        return [acts[-1]]
    sc = v.run(lambda e: e.tensor_tensor_scan(
        Hb[0:32, 0:T], AL[:, 0:T], BE[:, 0:T], 0.0,
        ALU.mult, ALU.add), *acts)
    hc = sc
    if _TRIM == 1:
        return [sc]

    outtok = []
    prevld = [None, None]
    prevmm = None
    for c in range(NCH):
        for j in range(NT):
            b = j % 2
            buf = AF32_[:, TMP0 + b*512: TMP0 + b*512 + 512]
            tmp2 = AF32_[:, TMP0 + 1024 + b*512: TMP0 + 1024 + b*512 + 512]
            extra = [sc] if (c == 0 and j < 2) else []
            mm = t.run(lambda e, c=c, j=j: e.matmul(
                ps_out[:, 0:512], WoutT[0:32, c*128:(c+1)*128],
                Hb[0:32, j*512:(j+1)*512], start=True, stop=True), hc, prevmm)
            ld = q.run(lambda e, c=c, j=j, b=b: e.dma_start(
                out=buf[:, 0:512], in_=src_dram[c*128:(c+1)*128, j*512:(j+1)*512]),
                prevld[b], *extra)
            e1 = v.run(lambda e, c=c, b=b: e.tensor_tensor(
                tmp2[:, 0:512], ps_out[:, 0:512], buf[:, 0:512], ALU.add),
                mm, ld, *extra)
            e1 = v.run(lambda e, c=c, b=b: e.tensor_scalar(
                tmp2[:, 0:512], tmp2[:, 0:512], BIAS[:, 2+c:3+c], None,
                ALU.add), e1)
            prevmm = e1
            e2 = v.run(lambda e, c=c, j=j, b=b: e.tensor_tensor(
                buf[:, 0:512], tmp2[:, 0:512], bld.xn(c, j*512, 512), ALU.add), e1)
            st = q.run(lambda e, c=c, j=j, b=b: e.dma_start(
                out=dst_dram[c*128:(c+1)*128, j*512:(j+1)*512], in_=buf[:, 0:512]), e2)
            prevld[b] = st
            outtok.append(st)
    return outtok



def emit_l0_ffn(bld, src_dram, dst_dram, after=()):
    t, v, s, q, gd, g = bld.t, bld.v, bld.s, bld.q, bld.gd, bld.g
    AB, AF32_ = bld.ABF, bld.AF32
    io = bld.io
    NE = 10

    OUT_d = bld.nc.dram_tensor("l0f_out_d", [D, T], F32, kind="Internal").ap()
    gfm_d = bld.nc.dram_tensor("l0f_gfm_d", [8, T], F32, kind="Internal").ap()

    RW = AB[:, WRK:WRK+64]
    lw = [gd.run(lambda e, c=c: e.dma_start(out=RW[:, c*8:(c+1)*8],
          in_=io['l0f_rw'][c*128:(c+1)*128, :])) for c in range(NCH)]
    RB = AF32_[:, MISC+160:MISC+161]
    lw.append(q.run(lambda e: e.dma_start(out=RB[0:8, 0:1], in_=io['l0f_rb'][:, :])))
    LG = AF32_[0:8, TMP1:TMP1+T]
    psR, psT_, psG, ps1, ps3, psO = (bld.pb[0], bld.pb[1], bld.pb[3],
                                     bld.pb[4], bld.pb[5], bld.pb[6])
    lgt = []
    prev_e = None
    for j in range(NT):
        mm = None
        for c in range(NCH):
            mm = t.run(lambda e, c=c, j=j: e.matmul(
                psR[0:8, 0:512], RW[:, c*8:(c+1)*8], bld.xn(c, j*512, 512),
                start=(c == 0), stop=(c == NCH-1)), lw, list(after),
                *( [prev_e] if prev_e else []))
        ev = s.run(lambda e, j=j: e.activation(
            out=LG[:, j*512:(j+1)*512], in_=psR[0:8, 0:512],
            func=AF_.Identity, bias=RB[0:8, 0:1]), mm)
        prev_e = ev
        lgt.append(ev)

    GA = AF32_[:, TMP0:TMP0+1024]
    LGT, V1, V2, NV1, REC = GA[:, 0:256], GA[:, 256:288], GA[:, 288:320], \
        GA[:, 320:352], GA[:, 352:384]
    E8, K8 = GA[:, 384:392], GA[:, 392:400]
    GT = GA[:, 512:768]
    trs = []
    for tt in range(NFOLD):
        trs.append(t.run(lambda e, tt=tt: e.matmul(
            psT_[:, tt*8:(tt+1)*8], LG[:, tt*128:(tt+1)*128],
            bld.ident[0:8, 0:8], start=True, stop=True), lgt[tt//4],
            trs[-1] if tt else list(after)))
    ec = v.run(lambda e: e.tensor_copy(LGT[:, :], psT_[:, 0:256]), trs[-1],
               list(after))
    gts = []
    for tt in range(NFOLD):
        tk = v.run(lambda e, tt=tt: e.reduce_max(
            V1[:, tt:tt+1], LGT[:, tt*8:(tt+1)*8], axis=AXX), ec)
        tk = v.run(lambda e, tt=tt: e.tensor_scalar(
            E8[:, 0:8], LGT[:, tt*8:(tt+1)*8], V1[:, tt:tt+1], None, ALU.is_ge), tk)
        tk = v.run(lambda e, tt=tt: e.scalar_tensor_tensor(
            K8[:, 0:8], E8[:, 0:8], -1e30, LGT[:, tt*8:(tt+1)*8],
            ALU.mult, ALU.add), tk)
        tk = v.run(lambda e, tt=tt: e.reduce_max(
            V2[:, tt:tt+1], K8[:, 0:8], axis=AXX), tk)
        gts.append(tk)
    nv = v.run(lambda e: e.tensor_scalar(NV1[:, :], V1[:, :], -1.0, None,
                                         ALU.mult), gts)
    dd = v.run(lambda e: e.tensor_tensor(REC[:, :], V2[:, :], NV1[:, :],
                                         ALU.add), nv)
    dd = s.run(lambda e: e.activation(out=REC[:, :], in_=REC[:, :],
                                      func=AF_.Exp), dd)
    dd = v.run(lambda e: e.tensor_scalar(REC[:, :], REC[:, :], 1.0, None,
                                         ALU.add), dd)
    dd = v.run(lambda e: e.reciprocal(REC[:, :], REC[:, :]), dd)
    GTb = AB[:, SQ0:SQ0+256]
    gcast = []
    for tt in range(NFOLD):
        tk = s.run(lambda e, tt=tt: e.activation(
            out=GT[:, tt*8:(tt+1)*8], in_=LGT[:, tt*8:(tt+1)*8],
            func=AF_.Exp, bias=NV1[:, tt:tt+1]), dd, gts)
        tk = v.run(lambda e, tt=tt: e.tensor_scalar(
            E8[:, 0:8], LGT[:, tt*8:(tt+1)*8], V2[:, tt:tt+1], None,
            ALU.is_ge), tk, gcast[-1] if gcast else None)
        tk = v.run(lambda e, tt=tt: e.tensor_tensor(
            GT[:, tt*8:(tt+1)*8], GT[:, tt*8:(tt+1)*8], E8[:, 0:8],
            ALU.mult), tk)
        tk = v.run(lambda e, tt=tt: e.tensor_scalar(
            GTb[:, tt*8:(tt+1)*8], GT[:, tt*8:(tt+1)*8], REC[:, tt:tt+1],
            None, ALU.mult), tk)
        gcast.append(tk)
    GFM = AF32_[0:8, TMP1:TMP1+T]
    gfm_toks = []
    for jj in range(NT):
        mm = None
        for k in range(4):
            tt = jj*4 + k
            mm = t.run(lambda e, tt=tt, k=k: e.matmul(
                psT_[0:8, k*128:(k+1)*128], GTb[:, tt*8:(tt+1)*8],
                bld.identb[:, :], start=True, stop=True), gcast[tt],
                gfm_toks[-1] if gfm_toks else ec)
        gfm_toks.append(v.run(lambda e, jj=jj: e.tensor_copy(
            GFM[:, jj*512:(jj+1)*512], psT_[0:8, 0:512]), mm, lgt))
    stg = q.run(lambda e: e.dma_start(out=gfm_d[0:8, :], in_=GFM[:, :]),
                gfm_toks)

    W1 = AB[:, WRK+64:WRK+64+4096]
    W3 = AB[:, WRK+4160:WRK+4160+4096]
    W2 = AB[:, WRK+8256:WRK+8256+4096]
    HH = AB[:, SQ0+256:SQ0+256+2048]
    Sb = AB[:, SQ0+2304:SQ0+2304+512]
    OUT_sb = AF32_[:, TMP0:TMP0+512]
    HT = AF32_[:, TMP0+512:TMP0+1024]
    acc = {}
    wprev = [None]
    grow_prev = [stg]
    gate_last = [None]
    out_cp_prev = [None]
    for e_i in range(NE):
        wl = []
        wdep = [wprev[0]] if wprev[0] else [list(after)]
        for c in range(NCH):
            wl.append(gd.run(lambda e, e_i=e_i, c=c: e.dma_start(
                out=W1[:, c*512:(c+1)*512],
                in_=io['l0f_w1'][e_i][c*128:(c+1)*128, :]), *wdep))
            wl.append(gd.run(lambda e, e_i=e_i, c=c: e.dma_start(
                out=W3[:, c*512:(c+1)*512],
                in_=io['l0f_w3'][e_i][c*128:(c+1)*128, :])))
        for fc in range(4):
            wl.append(gd.run(lambda e, e_i=e_i, fc=fc: e.dma_start(
                out=W2[:, fc*1024:(fc+1)*1024],
                in_=io['l0f_w2'][e_i][fc*128:(fc+1)*128, :])))
        routed = e_i < 8
        if routed:
            gl = q.run(lambda e, e_i=e_i: e.dma_start(
                out=bld.rowTa[0:1, 0:T], in_=gfm_d[e_i:e_i+1, :]), stg,
                grow_prev[0])
        lastmm = None
        for j in range(NT):
            if routed:
                gmm = t.run(lambda e, j=j: e.matmul(
                    psG[:, 0:512], bld.onesrow[0:1, :],
                    bld.rowTa[0:1, j*512:(j+1)*512], start=True, stop=True),
                    gl, gate_last[0])
            for fc in range(4):
                m1 = None
                for c in range(NCH):
                    m1 = t.run(lambda e, c=c, j=j, fc=fc: e.matmul(
                        ps1[:, 0:512], W1[:, c*512+fc*128:c*512+(fc+1)*128],
                        bld.xn(c, j*512, 512), start=(c == 0),
                        stop=(c == NCH-1)), wl, list(after),
                        *( [lastmm] if lastmm else []))
                    m1 = t.run(lambda e, c=c, j=j, fc=fc: e.matmul(
                        ps3[:, 0:512], W3[:, c*512+fc*128:c*512+(fc+1)*128],
                        bld.xn(c, j*512, 512), start=(c == 0),
                        stop=(c == NCH-1)))
                s1 = s.run(lambda e, fc=fc: e.activation(
                    out=Sb[:, 0:512], in_=ps1[:, 0:512], func=AF_.Silu), m1,
                    lastmm)
                if routed:
                    ht = v.run(lambda e, fc=fc: e.tensor_tensor(
                        HT[:, 0:512], ps3[:, 0:512], Sb[:, 0:512], ALU.mult),
                        s1, m1, out_cp_prev[0])
                    hh = v.run(lambda e, fc=fc: e.tensor_tensor(
                        HH[:, fc*512:(fc+1)*512], HT[:, 0:512], psG[:, 0:512],
                        ALU.mult), ht, gmm)
                else:
                    hh = v.run(lambda e, fc=fc: e.tensor_tensor(
                        HH[:, fc*512:(fc+1)*512], ps3[:, 0:512], Sb[:, 0:512],
                        ALU.mult), s1, m1, out_cp_prev[0])
                lastmm = hh
            gate_last[0] = lastmm
            for o in range(NCH):
                mo = None
                for fc in range(4):
                    mo = t.run(lambda e, o=o, fc=fc: e.matmul(
                        psO[:, 0:512], W2[:, fc*1024+o*128:fc*1024+(o+1)*128],
                        HH[:, fc*512:(fc+1)*512], start=(fc == 0),
                        stop=(fc == 3)), lastmm,
                        out_cp_prev[0] if out_cp_prev[0] else None)
                cp = v.run(lambda e: e.tensor_copy(OUT_sb[:, 0:512],
                                                   psO[:, 0:512]), mo)
                out_cp_prev[0] = cp
                key = (j, o)
                ac = gd.run(lambda e, j=j, o=o, e_i=e_i: e.dma_start(
                    out=OUT_d[o*128:(o+1)*128, j*512:(j+1)*512],
                    in_=OUT_sb[:, 0:512],
                    **({} if e_i == 0 else dict(accum_op=ALU.add))),
                    cp, acc.get(key))
                acc[key] = ac
                out_cp_prev[0] = ac
        wprev[0] = out_cp_prev[0]
        grow_prev[0] = gate_last[0]

    outtok = []
    prevld = [None, None]
    prevadd = None
    for c in range(NCH):
        for j in range(NT):
            b = j % 2
            buf = AF32_[:, TMP0 + b*1024: TMP0 + b*1024 + 512]
            buf2 = AF32_[:, TMP0 + b*1024 + 512: TMP0 + b*1024 + 1024]
            dep = [acc[(j, c)]]
            ld1 = q.run(lambda e, c=c, j=j, b=b: e.dma_start(
                out=buf[:, 0:512], in_=src_dram[c*128:(c+1)*128, j*512:(j+1)*512]),
                prevld[b], out_cp_prev[0])
            ld2 = gd.run(lambda e, c=c, j=j, b=b: e.dma_start(
                out=buf2[:, 0:512], in_=OUT_d[c*128:(c+1)*128, j*512:(j+1)*512]),
                dep, prevld[b])
            ad = v.run(lambda e, b=b: e.tensor_tensor(
                buf[:, 0:512], buf[:, 0:512], buf2[:, 0:512], ALU.add),
                ld1, ld2, prevadd)
            prevadd = ad
            st = q.run(lambda e, c=c, j=j, b=b: e.dma_start(
                out=dst_dram[c*128:(c+1)*128, j*512:(j+1)*512], in_=buf[:, 0:512]),
                ad)
            prevld[b] = st
            outtok.append(st)
    return outtok



def emit_l1_ffn(bld, src_dram, dst_dram, after=()):
    t, v, s, q, gd, g = bld.t, bld.v, bld.s, bld.q, bld.gd, bld.g
    AB, AF32_ = bld.ABF, bld.AF32
    io = bld.io
    h_d = bld.nc.dram_tensor("l1f_h_d", [D, T], F32, kind="Internal").ap()

    W0 = AB[:, WRK:WRK+8192]
    W1_ = AB[:, WRK+8192:WRK+16384]
    LA = AB[:, WRK+16384:WRK+16896]
    LB = AB[0:32, WRK+16896:WRK+18944]
    G8 = AF32_[0:8, MISC+320:MISC+448]
    GBo = AB[:, WRK+19072:WRK+23168]
    YB = AB[:, WRK+23168:WRK+27264]
    SQB = AB[:, WRK+27264:WRK+27776]
    RB = AB[0:32, WRK+27776:WRK+28288]
    HB = AB[:, SQ0:SQ0+4096]
    HT32 = AF32_[:, TMP0:TMP0+4096]
    Y32 = AF32_[:, TMP1:TMP1+4096]
    QF = AF32_[:, TMP0:TMP0+1024]
    SC8 = AF32_[0:8, TMP0+1024:TMP0+2048]
    BIASALL = AF32_[:, MISC+160:MISC+208]
    GBALL = AF32_[:, MISC+208:MISC+256]
    STAT = AF32_[:, MISC+256:MISC+320]
    QI = bld.AI32b

    psDQ1, psDQ2 = bld.pb[6], bld.pb[7]
    psY = bld.pb[1]
    psR = bld.pb[2]
    psBC1, psBC2 = bld.pb[3], bld.pb[4]
    psST = bld.pb[0]
    psTR = bld.pb[5]

    h_prev = {}
    ldg = q.run(lambda e: e.dma_start(out=G8[0:8, 0:128], in_=io['l1f_g8'][:, :]))
    ldb = q.run(lambda e: e.dma_start(out=BIASALL[:, 0:48], in_=io['l1f_bias'][:, :]))
    ldgb = q.run(lambda e: e.dma_start(out=GBALL[:, 0:48], in_=io['l1f_gb'][:, :]))
    base = [ldg, ldb, ldgb] + list(after)

    def load_mat(mi, dst, dep):
        tks = list(dep)
        prev = None
        for c in range(NCH):
            lq = gd.run(lambda e, mi=mi, c=c: e.dma_start(
                out=QI[:, 0:1024], in_=io['l1f_qT'][mi][c*128:(c+1)*128, :]),
                tks, prev)
            ls = q.run(lambda e, mi=mi, c=c: e.dma_start(
                out=SC8[0:8, 0:1024], in_=io['l1f_scT'][mi][c*8:(c+1)*8, :]),
                tks, prev)
            m1 = t.run(lambda e: e.matmul(psDQ1[:, 0:512], G8[0:8, 0:128],
                       SC8[0:8, 0:512], start=True, stop=True), ls, ldg, prev)
            m2 = t.run(lambda e: e.matmul(psDQ2[:, 0:512], G8[0:8, 0:128],
                       SC8[0:8, 512:1024], start=True, stop=True), ls)
            cv = v.run(lambda e: e.tensor_copy(QF[:, 0:1024], QI[:, 0:1024]),
                       lq, prev)
            cv = v.run(lambda e: e.tensor_scalar(QF[:, 0:1024], QF[:, 0:1024],
                       1.0/7.5, -1.0, ALU.mult, ALU.add), cv)
            w1 = v.run(lambda e, c=c: e.tensor_tensor(
                dst[:, c*1024:c*1024+512], QF[:, 0:512], psDQ1[:, 0:512],
                ALU.mult), cv, m1)
            w2 = v.run(lambda e, c=c: e.tensor_tensor(
                dst[:, c*1024+512:(c+1)*1024], QF[:, 512:1024], psDQ2[:, 0:512],
                ALU.mult), w1, m2)
            prev = w2
        return prev

    wlast = [None]
    for blk in range(3):
        i0, i1 = 2*blk, 2*blk + 1
        dep = [wlast[0]] if wlast[0] else base
        dq0 = load_mat(i0, W0, dep)
        dq1 = load_mat(i1, W1_, [dq0])
        lal = []
        for m, mi in ((0, i0), (1, i1)):
            for c in range(NCH):
                lal.append(gd.run(lambda e, m=m, mi=mi, c=c: e.dma_start(
                    out=LA[:, (m*8+c)*32:(m*8+c+1)*32],
                    in_=io['l1f_laT'][mi][c*128:(c+1)*128, :]), dep))
            lal.append(gd.run(lambda e, m=m, mi=mi: e.dma_start(
                out=LB[0:32, m*1024:(m+1)*1024], in_=io['l1f_lbT'][mi][:, :]),
                dep))
        wready = [dq1] + lal + base

        tile_prev = [None]
        for j in range(NT):
            tile_sts = []
            hdeps = []
            if blk == 0:
                hb = lambda c, j=j: bld.xn(c, j*512, 512)
            else:
                for c in range(NCH):
                    ldh = q.run(lambda e, c=c, j=j: e.dma_start(
                        out=HT32[:, c*512:(c+1)*512],
                        in_=h_d[c*128:(c+1)*128, j*512:(j+1)*512]),
                        h_prev.get((c, j)), tile_prev[0], dq1)
                    cvh = v.run(lambda e, c=c: e.tensor_copy(
                        HB[:, c*512:(c+1)*512], HT32[:, c*512:(c+1)*512]), ldh,
                        tile_prev[0])
                    hdeps.append(cvh)
                hb = lambda c: HB[:, c*512:(c+1)*512]
            mr = None
            for c in range(NCH):
                mr = t.run(lambda e, c=c: e.matmul(
                    psR[0:32, 0:512], LA[:, c*32:(c+1)*32], hb(c),
                    start=(c == 0), stop=(c == NCH-1)), wready, hdeps,
                    tile_prev[0])
            r1b = v.run(lambda e: e.tensor_copy(RB[0:32, 0:512],
                                                psR[0:32, 0:512]), mr)
            stat_mms = []
            ep_prev = None
            for o in range(NCH):
                mo = None
                for c in range(NCH):
                    mo = t.run(lambda e, c=c, o=o: e.matmul(
                        psY[:, 0:512], W0[:, c*1024+o*128:c*1024+(o+1)*128],
                        hb(c), start=(c == 0), stop=False),
                        (r1b, ep_prev) if c == 0 else None)
                mo = t.run(lambda e, o=o: e.matmul(
                    psY[:, 0:512], LB[0:32, o*128:(o+1)*128], RB[0:32, 0:512],
                    start=False, stop=True), mo)
                e1 = v.run(lambda e, o=o: e.tensor_scalar(
                    Y32[:, o*512:(o+1)*512], psY[:, 0:512],
                    BIASALL[:, i0*8+o:i0*8+o+1], None, ALU.add), mo, ep_prev)
                cb = v.run(lambda e, o=o: e.tensor_copy(
                    YB[:, o*512:(o+1)*512], Y32[:, o*512:(o+1)*512]), e1)
                sq = s.run(lambda e, o=o: e.activation(
                    out=SQB[:, 0:512], in_=YB[:, o*512:(o+1)*512],
                    func=AF_.Square), cb, stat_mms[-1] if stat_mms else None)
                ep_prev = cb
                for sub in range(4):
                    stat_mms.append(t.run(lambda e, o=o, sub=sub: e.matmul(
                        psST[:, o*4+sub:o*4+sub+1],
                        YB[:, o*512+sub*128:o*512+(sub+1)*128],
                        bld.onescol[:, 0:1], start=True, stop=True), cb))
                    stat_mms.append(t.run(lambda e, o=o, sub=sub: e.matmul(
                        psST[:, 32+o*4+sub:32+o*4+sub+1],
                        SQB[:, sub*128:(sub+1)*128],
                        bld.onescol[:, 0:1], start=True, stop=True), sq))
            tk = v.run(lambda e: e.tensor_copy(STAT[:, 0:4], psST[:, 0:4]),
                       stat_mms)
            for o in range(1, NCH):
                tk = v.run(lambda e, o=o: e.tensor_tensor(
                    STAT[:, 0:4], STAT[:, 0:4], psST[:, o*4:(o+1)*4], ALU.add), tk)
            tk2 = v.run(lambda e: e.tensor_copy(STAT[:, 4:8], psST[:, 32:36]), tk)
            for o in range(1, NCH):
                tk2 = v.run(lambda e, o=o: e.tensor_tensor(
                    STAT[:, 4:8], STAT[:, 4:8], psST[:, 32+o*4:32+(o+1)*4],
                    ALU.add), tk2)
            tk = v.run(lambda e: e.tensor_scalar(STAT[:, 0:4], STAT[:, 0:4],
                       1.0/D, None, ALU.mult), tk)
            tk3 = v.run(lambda e: e.tensor_tensor(STAT[:, 12:16], STAT[:, 0:4],
                       STAT[:, 0:4], ALU.mult), tk)
            tk3 = v.run(lambda e: e.scalar_tensor_tensor(
                STAT[:, 4:8], STAT[:, 4:8], 1.0/D, STAT[:, 12:16],
                ALU.mult, ALU.subtract), tk3, tk2)
            tk3 = s.run(lambda e: e.activation(out=STAT[:, 12:16],
                       in_=STAT[:, 4:8], func=AF_.Sqrt,
                       bias=bld.constf[:, 0:1]), tk3)
            tka = v.run(lambda e: e.reciprocal(STAT[:, 8:12], STAT[:, 12:16]), tk3)
            tkb = v.run(lambda e: e.tensor_tensor(STAT[:, 12:16], STAT[:, 0:4],
                       STAT[:, 8:12], ALU.mult), tka)
            tkb = v.run(lambda e: e.tensor_scalar(STAT[:, 12:16], STAT[:, 12:16],
                       -1.0, None, ALU.mult), tkb)
            ra = fold_to_row(bld, STAT[:, 8:12], bld.rowTa[0:1, 0:512],
                             bld.bounce, psTR, after=[tka],
                             war=[tile_prev[0]] if tile_prev[0] else list(after),
                             k=4)
            rb2 = fold_to_row(bld, STAT[:, 12:16], bld.rowTb[0:1, 0:512],
                              bld.bounce2, psTR, after=[tkb, ra], k=4)
            bca = t.run(lambda e: e.matmul(psBC1[:, 0:512],
                       bld.onesrow[0:1, :], bld.rowTa[0:1, 0:512],
                       start=True, stop=True), ra, tile_prev[0])
            bcb = t.run(lambda e: e.matmul(psBC2[:, 0:512],
                       bld.onesrow[0:1, :], bld.rowTb[0:1, 0:512],
                       start=True, stop=True), rb2)
            gel = []
            for o in range(NCH):
                m1 = v.run(lambda e, o=o: e.tensor_tensor(
                    Y32[:, o*512:(o+1)*512], Y32[:, o*512:(o+1)*512],
                    psBC1[:, 0:512], ALU.mult), bca, tk3,
                    gel[-1] if gel else tkb)
                m2 = v.run(lambda e, o=o: e.tensor_tensor(
                    Y32[:, o*512:(o+1)*512], Y32[:, o*512:(o+1)*512],
                    psBC2[:, 0:512], ALU.add), m1, bcb)
                m3 = v.run(lambda e, o=o: e.tensor_scalar(
                    Y32[:, o*512:(o+1)*512], Y32[:, o*512:(o+1)*512],
                    GBALL[:, blk*16+o:blk*16+o+1],
                    GBALL[:, blk*16+8+o:blk*16+8+o+1], ALU.mult, ALU.add), m2)
                g1 = s.run(lambda e, o=o: e.activation(
                    out=GBo[:, o*512:(o+1)*512], in_=Y32[:, o*512:(o+1)*512],
                    func=AF_.Gelu), m3)
                gel.append(g1)
            mr2 = None
            for c in range(NCH):
                mr2 = t.run(lambda e, c=c: e.matmul(
                    psR[0:32, 0:512], LA[:, (8+c)*32:(8+c+1)*32],
                    GBo[:, c*512:(c+1)*512], start=(c == 0),
                    stop=(c == NCH-1)), gel, r1b)
            r2b = v.run(lambda e: e.tensor_copy(RB[0:32, 0:512],
                        psR[0:32, 0:512]), mr2, gel)
            last = None
            for o in range(NCH):
                mo = None
                for c in range(NCH):
                    mo = t.run(lambda e, c=c, o=o: e.matmul(
                        psY[:, 0:512], W1_[:, c*1024+o*128:c*1024+(o+1)*128],
                        GBo[:, c*512:(c+1)*512], start=(c == 0), stop=False),
                        (r2b, last) if c == 0 else None)
                mo = t.run(lambda e, o=o: e.matmul(
                    psY[:, 0:512], LB[0:32, 1024+o*128:1024+(o+1)*128],
                    RB[0:32, 0:512], start=False, stop=True), mo)
                e1 = v.run(lambda e, o=o: e.tensor_scalar(
                    Y32[:, o*512:(o+1)*512], psY[:, 0:512],
                    BIASALL[:, i1*8+o:i1*8+o+1], None, ALU.add), mo,
                    last, gel[o])
                if blk == 0:
                    e2 = v.run(lambda e, o=o, j=j: e.tensor_tensor(
                        Y32[:, o*512:(o+1)*512], Y32[:, o*512:(o+1)*512],
                        bld.xn(o, j*512, 512), ALU.add), e1)
                else:
                    e2 = v.run(lambda e, o=o: e.tensor_tensor(
                        Y32[:, o*512:(o+1)*512], Y32[:, o*512:(o+1)*512],
                        HT32[:, o*512:(o+1)*512], ALU.add), e1)
                st = q.run(lambda e, o=o, j=j: e.dma_start(
                    out=h_d[o*128:(o+1)*128, j*512:(j+1)*512],
                    in_=Y32[:, o*512:(o+1)*512]), e2, h_prev.get((o, j)))
                h_prev[(o, j)] = st
                tile_sts.append(st)
                last = e2
            tile_prev[0] = [last] + tile_sts
        wlast[0] = tile_prev[0]

    outtok = []
    prevld = [None, None]
    prevadd = wlast[0]
    for c in range(NCH):
        for j in range(NT):
            b = j % 2
            buf = AF32_[:, TMP0 + b*1024: TMP0 + b*1024 + 512]
            buf2 = AF32_[:, TMP0 + b*1024 + 512: TMP0 + b*1024 + 1024]
            ld1 = q.run(lambda e, c=c, j=j, b=b: e.dma_start(
                out=buf[:, 0:512], in_=src_dram[c*128:(c+1)*128, j*512:(j+1)*512]),
                prevld[b], wlast[0])
            ld2 = gd.run(lambda e, c=c, j=j, b=b: e.dma_start(
                out=buf2[:, 0:512], in_=h_d[c*128:(c+1)*128, j*512:(j+1)*512]),
                h_prev[(c, j)], prevld[b], wlast[0])
            ad = v.run(lambda e, b=b: e.tensor_tensor(
                buf[:, 0:512], buf[:, 0:512], buf2[:, 0:512], ALU.add),
                ld1, ld2, prevadd)
            prevadd = ad
            st = q.run(lambda e, c=c, j=j, b=b: e.dma_start(
                out=dst_dram[c*128:(c+1)*128, j*512:(j+1)*512], in_=buf[:, 0:512]),
                ad)
            prevld[b] = st
            outtok.append(st)
    return outtok


def build_l1f():
    nc = bass.Bass("TRN2", target_bir_lowering=False)
    io = {}
    io['x_in'] = nc.dram_tensor("x_in", [D, T], F32, kind="ExternalInput").ap()
    io['consts_in'] = nc.dram_tensor("consts_in", [128, 266], F32, kind="ExternalInput").ap()
    io['l1f_qT'] = [nc.dram_tensor(f"l1f_qT_{i}", [D, D], I32, kind="ExternalInput").ap() for i in range(6)]
    io['l1f_scT'] = [nc.dram_tensor(f"l1f_scT_{i}", [64, D], F32, kind="ExternalInput").ap() for i in range(6)]
    io['l1f_laT'] = [nc.dram_tensor(f"l1f_laT_{i}", [D, 32], F32, kind="ExternalInput").ap() for i in range(6)]
    io['l1f_lbT'] = [nc.dram_tensor(f"l1f_lbT_{i}", [32, D], F32, kind="ExternalInput").ap() for i in range(6)]
    io['l1f_bias'] = nc.dram_tensor("l1f_bias", [128, 48], F32, kind="ExternalInput").ap()
    io['l1f_gb'] = nc.dram_tensor("l1f_gb", [128, 48], F32, kind="ExternalInput").ap()
    io['l1f_g8'] = nc.dram_tensor("l1f_g8", [8, 128], F32, kind="ExternalInput").ap()
    io['y_out'] = nc.dram_tensor("y_out", [D, T], F32, kind="ExternalOutput").ap()
    es = ExitStack()
    with es:
        bld = Bld(nc, es)
        bld.init(io)
        block = es.enter_context(nc.Block())

        def prog(_):
            ct = bld.load_consts()
            ln = emit_ln_xn(bld, io['x_in'], after=ct)
            emit_l1_ffn(bld, io['x_in'], io['y_out'], after=ln)
            bld.q.wait(bld.q.tok(), bld.gd.tok())
        block.sync(prog)
    return nc


def host_l1f_inputs(ins):
    qw = np.asarray(ins['L1_ffn_qweights'])
    sc = np.asarray(ins['L1_ffn_scales'], np.float32)
    la = np.asarray(ins['L1_ffn_lora_a'], np.float32)
    lb = np.asarray(ins['L1_ffn_lora_b'], np.float32)
    bi = np.asarray(ins['L1_ffn_biases'], np.float32)
    ga = np.asarray(ins['L1_ffn_ln_gamma'], np.float32)
    be = np.asarray(ins['L1_ffn_ln_beta'], np.float32)
    d = dict(consts_in=consts_array())
    for i in range(6):
        d[f'l1f_qT_{i}'] = np.ascontiguousarray(qw[i].T)
        d[f'l1f_scT_{i}'] = np.ascontiguousarray(sc[i].T)
        d[f'l1f_laT_{i}'] = np.ascontiguousarray(la[i].T)
        d[f'l1f_lbT_{i}'] = np.ascontiguousarray(lb[i].T)
    bias = np.zeros((128, 48), np.float32)
    gb = np.zeros((128, 48), np.float32)
    for o in range(NCH):
        for i in range(6):
            bias[:, i*8+o] = bi[i][o*128:(o+1)*128]
        for b in range(3):
            gb[:, b*16+o] = ga[b][o*128:(o+1)*128]
            gb[:, b*16+8+o] = be[b][o*128:(o+1)*128]
    d['l1f_bias'] = bias
    d['l1f_gb'] = gb
    g8 = np.zeros((8, 128), np.float32)
    for p in range(128):
        g8[p // 16, p] = 1.0
    d['l1f_g8'] = g8
    return d



def emit_l3a(bld, src_dram, dst_dram, after=()):
    t, v, s, q, gd, g = bld.t, bld.v, bld.s, bld.q, bld.gd, bld.g
    AB, AF32_ = bld.ABF, bld.AF32
    io = bld.io

    WQKV = AB[:, WRK:WRK+24576]
    WPK = AB[:, WRK:WRK+8192]
    SEL16 = AB[:, WRK+24576:WRK+24704]
    SEL2 = AB[:, WRK+24704:WRK+24706]
    REPS = AB[0:2, WRK+24706:WRK+24834]
    QSfm = AB[0:16, WRK+24834:WRK+28930]
    WKb = AB[0:16, WRK+28930:WRK+29954]
    QEb = AB[:, SQ0:SQ0+512]
    KEbt = AB[:, SQ0+512:SQ0+1024]
    Vb = AB[:, SQ0+1024:SQ0+1536]
    Vsb = AB[0:2, SQ0+1536:SQ0+2048]
    KVS = AF32_[:, MISC+160:MISC+176]
    BP = AF32_[:, MISC+176:MISC+184]
    E1 = AF32_[:, TMP0:TMP0+512]
    E2 = AF32_[:, TMP0+512:TMP0+1024]

    psQ, psK, psV, psS2, psREP, psW, psQS, psY = (
        bld.pb[0], bld.pb[1], bld.pb[2], bld.pb[3], bld.pb[4], bld.pb[5],
        bld.pb[6], bld.pb[7])

    lw = []
    for c in range(NCH):
        lw.append(gd.run(lambda e, c=c: e.dma_start(
            out=WQKV[:, c*3072:(c+1)*3072],
            in_=io['l3a_wqkv'][c*128:(c+1)*128, :]),
            list(after) if c == 0 else None))
    lw.append(gd.run(lambda e: e.dma_start(out=SEL16[:, 0:128], in_=io['l3a_sel16'][:, :])))
    lw.append(gd.run(lambda e: e.dma_start(out=SEL2[:, 0:2], in_=io['l3a_sel2'][:, :])))
    lw.append(gd.run(lambda e: e.dma_start(out=REPS[0:2, 0:128], in_=io['l3a_reps'][:, :])))
    lw.append(q.run(lambda e: e.dma_start(out=BP[:, 0:8], in_=io['l3a_bp'][:, :])))

    kz = v.run(lambda e: e.memset(KVS[:, 0:9], 0.0), list(after))
    qs_ev = []
    chain = [kz]
    for j in range(NT):
        mmqs = None
        for p in range(8):
            mq = mk = mv = None
            for c in range(NCH):
                mq = t.run(lambda e, c=c, j=j, p=p: e.matmul(
                    psQ[:, 0:512], WQKV[:, c*3072 + p*128: c*3072 + (p+1)*128],
                    bld.xn(c, j*512, 512), start=(c == 0), stop=(c == NCH-1)),
                    lw, list(after), chain[0])
                mk = t.run(lambda e, c=c, j=j, p=p: e.matmul(
                    psK[:, 0:512],
                    WQKV[:, c*3072 + 1024 + p*128: c*3072 + 1024 + (p+1)*128],
                    bld.xn(c, j*512, 512), start=(c == 0), stop=(c == NCH-1)))
                mv = t.run(lambda e, c=c, j=j, p=p: e.matmul(
                    psV[:, 0:512],
                    WQKV[:, c*3072 + 2048 + p*128: c*3072 + 2048 + (p+1)*128],
                    bld.xn(c, j*512, 512), start=(c == 0), stop=(c == NCH-1)))
            t1 = v.run(lambda e: e.tensor_scalar(E1[:, 0:512], psQ[:, 0:512],
                       0.0, None, ALU.min), mq, chain[0])
            t2 = s.run(lambda e: e.activation(out=E1[:, 0:512], in_=E1[:, 0:512],
                       func=AF_.Exp), t1)
            t3 = v.run(lambda e: e.tensor_scalar(E2[:, 0:512], psQ[:, 0:512],
                       0.0, None, ALU.max), t1, chain[0])
            t4 = v.run(lambda e: e.tensor_tensor(QEb[:, 0:512], E1[:, 0:512],
                       E2[:, 0:512], ALU.add), t3, t2, mmqs)
            k1 = v.run(lambda e: e.tensor_scalar(E1[:, 0:512], psK[:, 0:512],
                       0.0, None, ALU.min), mk, t4)
            k2 = s.run(lambda e: e.activation(out=E1[:, 0:512], in_=E1[:, 0:512],
                       func=AF_.Exp), k1, t4)
            k3 = v.run(lambda e: e.tensor_scalar(E2[:, 0:512], psK[:, 0:512],
                       0.0, None, ALU.max), k1, t4)
            k4 = v.run(lambda e: e.tensor_tensor(KEbt[:, 0:512], E1[:, 0:512],
                       E2[:, 0:512], ALU.add), k3, k2)
            v1 = v.run(lambda e: e.tensor_copy(Vb[:, 0:512], psV[:, 0:512]),
                       mv, k4)
            mmqs = t.run(lambda e, p=p: e.matmul(
                psQS[0:16, 0:512], SEL16[:, p*16:(p+1)*16], QEb[:, 0:512],
                start=(p == 0), stop=(p == 7)), t4, lw,
                qs_ev[-1] if (p == 0 and qs_ev) else None)
            mvs = t.run(lambda e: e.matmul(psS2[0:2, 0:512], SEL2[:, 0:2],
                        Vb[:, 0:512], start=True, stop=True), v1, chain[0])
            vsev = v.run(lambda e: e.tensor_copy(Vsb[0:2, 0:512],
                         psS2[0:2, 0:512]), mvs, v1)
            mrep = t.run(lambda e: e.matmul(psREP[:, 0:512], REPS[0:2, 0:128],
                         Vsb[0:2, 0:512], start=True, stop=True), vsev, chain[0])
            prod = v.run(lambda e: e.tensor_tensor(E1[:, 0:512],
                         KEbt[:, 0:512], psREP[:, 0:512], ALU.mult), mrep, k4,
                         t2, k2)
            red = v.run(lambda e: e.tensor_reduce(KVS[:, 8:9], E1[:, 0:512],
                        axis=AXX, op=ALU.add), prod)
            acc = v.run(lambda e, p=p: e.tensor_tensor(KVS[:, p:p+1],
                        KVS[:, p:p+1], KVS[:, 8:9], ALU.add), red, kz)
            chain[0] = acc
        qs_ev.append(v.run(lambda e, j=j: e.tensor_copy(
            QSfm[0:16, j*512:(j+1)*512], psQS[0:16, 0:512]), mmqs, chain[0]))

    lwp = []
    for c in range(NCH):
        lwp.append(gd.run(lambda e, c=c: e.dma_start(
            out=WPK[:, c*1024:(c+1)*1024],
            in_=io['l3a_wp'][c*128:(c+1)*128, :]),
            [chain[0], qs_ev[-1]] if c == 0 else None))
    wks = []
    for c in range(NCH):
        wks.append(v.run(lambda e, c=c: e.tensor_scalar(
            WPK[:, c*1024:(c+1)*1024], WPK[:, c*1024:(c+1)*1024],
            KVS[:, c:c+1], None, ALU.mult), lwp, chain[0]))
    mmwk = None
    wkev0 = None
    for half in range(2):
        for c in range(NCH):
            mmwk = t.run(lambda e, c=c, half=half: e.matmul(
                psW[0:16, 0:512], SEL16[:, c*16:(c+1)*16],
                WPK[:, c*1024 + half*512: c*1024 + (half+1)*512],
                start=(c == 0), stop=(c == NCH-1)), wks,
                (mmwk, wkev0) if c == 0 else None)
        if half == 0:
            wkev0 = v.run(lambda e: e.tensor_copy(WKb[0:16, 0:512],
                          psW[0:16, 0:512]), mmwk)
    wkev = v.run(lambda e: e.tensor_copy(WKb[0:16, 512:1024],
                 psW[0:16, 0:512]), mmwk, wkev0)

    outtok = []
    prevld = [None, None]
    prevadd = wkev
    for c in range(NCH):
        for j in range(NT):
            b = j % 2
            buf = AF32_[:, TMP0 + b*1024: TMP0 + b*1024 + 512]
            buf2 = AF32_[:, TMP0 + b*1024 + 512: TMP0 + b*1024 + 1024]
            my = t.run(lambda e, c=c, j=j: e.matmul(
                psY[:, 0:512], WKb[0:16, c*128:(c+1)*128],
                QSfm[0:16, j*512:(j+1)*512], start=True, stop=True),
                wkev, prevadd)
            ld1 = q.run(lambda e, c=c, j=j, b=b: e.dma_start(
                out=buf[:, 0:512], in_=src_dram[c*128:(c+1)*128, j*512:(j+1)*512]),
                prevld[b], chain[0])
            e1 = v.run(lambda e, c=c, b=b: e.tensor_scalar(
                buf2[:, 0:512], psY[:, 0:512], BP[:, c:c+1], None, ALU.add),
                my, prevadd)
            e2 = v.run(lambda e, b=b: e.tensor_tensor(
                buf[:, 0:512], buf[:, 0:512], buf2[:, 0:512], ALU.add), e1, ld1)
            prevadd = e2
            st = q.run(lambda e, c=c, j=j, b=b: e.dma_start(
                out=dst_dram[c*128:(c+1)*128, j*512:(j+1)*512], in_=buf[:, 0:512]),
                e2)
            prevld[b] = st
            outtok.append(st)
    return outtok


def build_l3a():
    nc = bass.Bass("TRN2", target_bir_lowering=False)
    io = {}
    io['x_in'] = nc.dram_tensor("x_in", [D, T], F32, kind="ExternalInput").ap()
    io['consts_in'] = nc.dram_tensor("consts_in", [128, 266], F32, kind="ExternalInput").ap()
    io['l3a_wqkv'] = nc.dram_tensor("l3a_wqkv", [D, 3072], F32, kind="ExternalInput").ap()
    io['l3a_wp'] = nc.dram_tensor("l3a_wp", [D, D], F32, kind="ExternalInput").ap()
    io['l3a_bp'] = nc.dram_tensor("l3a_bp", [128, 8], F32, kind="ExternalInput").ap()
    io['l3a_sel16'] = nc.dram_tensor("l3a_sel16", [128, 128], F32, kind="ExternalInput").ap()
    io['l3a_sel2'] = nc.dram_tensor("l3a_sel2", [128, 2], F32, kind="ExternalInput").ap()
    io['l3a_reps'] = nc.dram_tensor("l3a_reps", [2, 128], F32, kind="ExternalInput").ap()
    io['y_out'] = nc.dram_tensor("y_out", [D, T], F32, kind="ExternalOutput").ap()
    es = ExitStack()
    with es:
        bld = Bld(nc, es)
        bld.init(io)
        block = es.enter_context(nc.Block())

        def prog(_):
            ct = bld.load_consts()
            ln = emit_ln_xn(bld, io['x_in'], after=ct)
            emit_l3a(bld, io['x_in'], io['y_out'], after=ln)
            bld.q.wait(bld.q.tok(), bld.gd.tok())
        block.sync(prog)
    return nc


def host_l3a_inputs(ins):
    bp = np.asarray(ins['L3_attn_b_proj'], np.float32)
    bpf = np.zeros((128, 8), np.float32)
    for c in range(NCH):
        bpf[:, c] = bp[c*128:(c+1)*128]
    sel16 = np.zeros((128, 128), np.float32)
    for p in range(8):
        for r in range(128):
            sel16[r, p*16 + 2*p + r//64] = 1.0
    sel16 = np.zeros((128, 128), np.float32)
    for p in range(8):
        for r in range(128):
            sel16[r, p*16 + 2*p + r//64] = 1.0
    sel2 = np.zeros((128, 2), np.float32)
    for r in range(128):
        sel2[r, r//64] = 1.0
    reps = sel2.T.copy()
    return dict(
        consts_in=consts_array(),
        l3a_wqkv=np.ascontiguousarray(np.asarray(ins['L3_attn_W_qkv'], np.float32)),
        l3a_wp=np.ascontiguousarray(np.asarray(ins['L3_attn_W_proj'], np.float32)),
        l3a_bp=bpf, l3a_sel16=sel16, l3a_sel2=sel2,
        l3a_reps=np.ascontiguousarray(reps),
    )



L0_CTX = 1536
L0_SCALE = 64.0 ** -0.5


def emit_l0a(bld, src_dram, dst_dram, after=()):
    t, v, s, q, gd, g = bld.t, bld.v, bld.s, bld.q, bld.gd, bld.g
    AB, AF32_ = bld.ABF, bld.AF32
    io = bld.io
    OUT_d = bld.nc.dram_tensor("l0a_out_d", [D, T], F32, kind="Internal").ap()
    bld.c['l0a_out_d'] = OUT_d

    COSB = AB[:, WRK:WRK+4096]
    SINB = AB[:, WRK+4096:WRK+8192]
    WQ = AB[:, WRK+8192:WRK+9216]
    WQP = AB[:, WRK+9216:WRK+10240]
    WKV = AB[:, WRK+10240:WRK+11264]
    WKVP = AB[:, WRK+11264:WRK+12288]
    QB = AB[:, WRK+12288:WRK+16384]
    LKF = AB[:, WRK+16384:WRK+20480]
    LKT = AB[:, WRK+20480:WRK+24576]
    PTB = AB[:, WRK+24576:WRK+28672]
    WOP = AB[:, WRK+28672:WRK+29696]
    MASKB = AB[:, WRK+29696:WRK+30848]
    OUTT = AB[:, WRK+30848:WRK+31360]
    SEL2 = AB[:, WRK+31360:WRK+31362]
    Pb = AB[:, SQ0:SQ0+4096]
    S_sb = AF32_[:, TMP0:TMP0+4096]
    E1 = AF32_[:, TMP1:TMP1+512]
    E2 = AF32_[:, TMP1+512:TMP1+1024]
    MS = AF32_[0:2, TMP1+1024:TMP1+1536]
    VS_ = AF32_[0:2, TMP1+1536:TMP1+2048]
    RS = AF32_[0:2, TMP1+2048:TMP1+2560]
    T2 = AF32_[0:2, TMP1+2560:TMP1+3072]
    REPSF = AF32_[0:2, TMP1+3072:TMP1+3200]
    GBC = AF32_[:, MISC+160:MISC+162]
    BO = AF32_[:, MISC+162:MISC+170]
    MXC = AF32_[:, MISC+170:MISC+174]

    psA, psB_, psC, psD, psE, psF, psG, psS = (
        bld.pb[0], bld.pb[1], bld.pb[2], bld.pb[3], bld.pb[4], bld.pb[5],
        bld.pb[6], bld.pb[7])
    psO = bld.pb[5]
    psY = bld.pb[4]

    lw0 = []
    lw0.append(gd.run(lambda e: e.dma_start(out=COSB[:, :], in_=io['l0a_cos'][:, :]),
               list(after)))
    lw0.append(gd.run(lambda e: e.dma_start(out=SINB[:, :], in_=io['l0a_sins'][:, :])))
    lw0.append(gd.run(lambda e: e.dma_start(out=MASKB[:, :], in_=io['l0a_maskb'][:, :])))
    lw0.append(gd.run(lambda e: e.dma_start(out=SEL2[:, 0:2], in_=io['l0a_sel2'][:, :])))
    lw0.append(q.run(lambda e: e.dma_start(out=REPSF[0:2, 0:128], in_=io['l0a_repsf'][:, :])))
    lw0.append(q.run(lambda e: e.dma_start(out=GBC[:, 0:2], in_=io['l0a_lngb'][:, :])))
    lw0.append(q.run(lambda e: e.dma_start(out=BO[:, 0:8], in_=io['l0a_bo'][:, :])))

    acc = {}
    pair_last = [None]
    for p in range(8):
        lw = []
        for c in range(NCH):
            for wdst, wsrc in ((WQ, 'l0a_wqT'), (WQP, 'l0a_wqTp'),
                               (WKV, 'l0a_wkvT'), (WKVP, 'l0a_wkvTp')):
                lw.append(gd.run(lambda e, c=c, wdst=wdst, wsrc=wsrc, p=p:
                    e.dma_start(out=wdst[:, c*128:(c+1)*128],
                    in_=io[wsrc][c*128:(c+1)*128, p*128:(p+1)*128]),
                    pair_last[0] if (c == 0 and wdst is WQ) else None))
        lw += lw0
        lwo = gd.run(lambda e, p=p: e.dma_start(
            out=WOP[:, 0:1024], in_=io['l0a_woT'][p*128:(p+1)*128, :]),
            pair_last[0])

        ch = [pair_last[0]]
        tmdone = []
        for j in range(NT):
            mq = mqp = mkv = mkvp = None
            for c in range(NCH):
                mq = t.run(lambda e, c=c, j=j: e.matmul(
                    psA[:, 0:512], WQ[:, c*128:(c+1)*128],
                    bld.xn(c, j*512, 512), start=(c == 0), stop=(c == NCH-1)),
                    lw, list(after), ch[0])
                mqp = t.run(lambda e, c=c, j=j: e.matmul(
                    psB_[:, 0:512], WQP[:, c*128:(c+1)*128],
                    bld.xn(c, j*512, 512), start=(c == 0), stop=(c == NCH-1)))
            e1 = v.run(lambda e, j=j: e.tensor_tensor(
                E1[:, 0:512], psA[:, 0:512], COSB[:, j*512:(j+1)*512], ALU.mult),
                mq, ch[0])
            e2 = v.run(lambda e, j=j: e.tensor_tensor(
                E2[:, 0:512], psB_[:, 0:512], SINB[:, j*512:(j+1)*512], ALU.mult),
                mqp, e1)
            e3 = v.run(lambda e, j=j: e.tensor_tensor(
                E1[:, 0:512], E1[:, 0:512], E2[:, 0:512], ALU.add), e2)
            qb = v.run(lambda e, j=j: e.tensor_scalar(
                QB[:, j*512:(j+1)*512], E1[:, 0:512], L0_SCALE, None, ALU.mult), e3)
            for c in range(NCH):
                mkv = t.run(lambda e, c=c, j=j: e.matmul(
                    psA[:, 0:512], WKV[:, c*128:(c+1)*128],
                    bld.xn(c, j*512, 512), start=(c == 0), stop=(c == NCH-1)),
                    e1 if c == 0 else None)
                mkvp = t.run(lambda e, c=c, j=j: e.matmul(
                    psB_[:, 0:512], WKVP[:, c*128:(c+1)*128],
                    bld.xn(c, j*512, 512), start=(c == 0), stop=(c == NCH-1)),
                    e2 if c == 0 else None)
            f1 = v.run(lambda e, j=j: e.tensor_tensor(
                E1[:, 0:512], psA[:, 0:512], COSB[:, j*512:(j+1)*512], ALU.mult),
                mkv, qb)
            f2 = v.run(lambda e, j=j: e.tensor_tensor(
                E2[:, 0:512], psB_[:, 0:512], SINB[:, j*512:(j+1)*512], ALU.mult),
                mkvp, f1)
            f3 = v.run(lambda e, j=j: e.tensor_tensor(
                E1[:, 0:512], E1[:, 0:512], E2[:, 0:512], ALU.add), f2)
            kvb = v.run(lambda e: e.tensor_copy(PTB[:, 0:512], E1[:, 0:512]), f3)
            sqb = s.run(lambda e: e.activation(out=PTB[:, 512:1024],
                       in_=PTB[:, 0:512], func=AF_.Square), kvb)
            ms1 = t.run(lambda e: e.matmul(psC[0:2, 0:512], SEL2[:, 0:2],
                       PTB[:, 0:512], start=True, stop=True), kvb, lw, ch[0])
            ms2 = t.run(lambda e: e.matmul(psD[0:2, 0:512], SEL2[:, 0:2],
                       PTB[:, 512:1024], start=True, stop=True), sqb, ch[0])
            r1 = v.run(lambda e: e.tensor_scalar(MS[0:2, 0:512], psC[0:2, 0:512],
                       1.0/64, None, ALU.mult), ms1, ch[0])
            r2 = v.run(lambda e: e.tensor_scalar(VS_[0:2, 0:512], psD[0:2, 0:512],
                       1.0/64, None, ALU.mult), ms2, ch[0])
            r3 = v.run(lambda e: e.tensor_tensor(T2[0:2, 0:512], MS[0:2, 0:512],
                       MS[0:2, 0:512], ALU.mult), r1)
            r4 = v.run(lambda e: e.tensor_tensor(VS_[0:2, 0:512], VS_[0:2, 0:512],
                       T2[0:2, 0:512], ALU.subtract), r3, r2)
            r5 = s.run(lambda e: e.activation(out=T2[0:2, 0:512], in_=VS_[0:2, 0:512],
                       func=AF_.Sqrt, bias=bld.constf[0:2, 0:1]), r4, sqb)
            r6 = v.run(lambda e: e.reciprocal(RS[0:2, 0:512], T2[0:2, 0:512]), r5)
            mb = t.run(lambda e: e.matmul(psE[:, 0:512], REPSF[0:2, 0:128],
                       MS[0:2, 0:512], start=True, stop=True), r1, ch[0], r3)
            rb = t.run(lambda e: e.matmul(psF[:, 0:512], REPSF[0:2, 0:128],
                       RS[0:2, 0:512], start=True, stop=True), r6, ch[0])
            u1 = v.run(lambda e: e.tensor_tensor(E2[:, 0:512], E1[:, 0:512],
                       psE[:, 0:512], ALU.subtract), mb, f3)
            u2 = v.run(lambda e: e.tensor_tensor(E1[:, 0:512], E2[:, 0:512],
                       psF[:, 0:512], ALU.mult), rb, u1)
            lk = v.run(lambda e, j=j: e.tensor_scalar(
                LKF[:, j*512:(j+1)*512], E1[:, 0:512], GBC[:, 0:1], GBC[:, 1:2],
                ALU.mult, ALU.add), u2)
            mt = None
            for sub in range(4):
                mt = t.run(lambda e, j=j, sub=sub: e.matmul(
                    psG[:, sub*128:(sub+1)*128],
                    LKF[:, j*512+sub*128:j*512+(sub+1)*128], bld.identb[:, :],
                    start=True, stop=True), lk, tmdone[-1] if tmdone else ch[0])
            tmdone.append(v.run(lambda e, j=j: e.tensor_copy(
                LKT[:, j*512:(j+1)*512], psG[:, 0:512]), mt))
            ch[0] = lk

        REMAP = list(range(12, 32)) + list(range(0, 12))
        prev_pt = [tmdone[-1]]
        prev_proj = [None]
        for gidx in range(8):
            outev = []
            for pqt in range(gidx*4, gidx*4+4):
                qt = REMAP[pqt]
                latent = qt >= 12
                kmax = 1536 + (qt-11)*128 if latent else 1536
                nkt = (kmax + 511) // 512
                nks = (kmax + 127) // 128
                for h in (0, 64):
                    sev = []
                    for kt in range(nkt):
                        mmS = t.run(lambda e, h=h, qt=qt, kt=kt: e.matmul(
                            psS[:, 0:512],
                            QB[h:h+64, qt*128:(qt+1)*128],
                            LKF[h:h+64, kt*512:(kt+1)*512],
                            start=True, stop=True), tmdone, prev_pt[0],
                            sev[-1] if sev else None)
                        b0 = 1536 + (qt-12)*128 - kt*512
                        if latent and b0 < 511:
                            off = 512 - b0
                            sev.append(v.run(lambda e, kt=kt, off=off: e.tensor_tensor(
                                S_sb[:, kt*512:(kt+1)*512], psS[:, 0:512],
                                MASKB[:, off:off+512], ALU.add), mmS, prev_pt[0]))
                        else:
                            sev.append(v.run(lambda e, kt=kt: e.tensor_copy(
                                S_sb[:, kt*512:(kt+1)*512], psS[:, 0:512]), mmS,
                                prev_pt[0]))
                    mx = v.run(lambda e, nkt=nkt: e.reduce_max(
                        MXC[:, 0:1], S_sb[:, 0:nkt*512], axis=AXX), sev)
                    nm = v.run(lambda e: e.tensor_scalar(MXC[:, 1:2], MXC[:, 0:1],
                              -1.0, None, ALU.mult), mx)
                    ex = s.run(lambda e, nkt=nkt: e.activation(
                        out=Pb[:, 0:nkt*512], in_=S_sb[:, 0:nkt*512],
                        func=AF_.Exp, bias=MXC[:, 1:2],
                        accum_out=MXC[:, 2:3]), nm, prev_pt[0])
                    rc = v.run(lambda e: e.reciprocal(MXC[:, 3:4], MXC[:, 2:3]), ex)
                    pn = v.run(lambda e, nkt=nkt: e.tensor_scalar(
                        Pb[:, 0:nkt*512], Pb[:, 0:nkt*512], MXC[:, 3:4], None,
                        ALU.mult), rc)
                    mmO = None
                    ptev = None
                    for ksg in range((nks + 3) // 4):
                        mt = None
                        for sub in range(ksg*4, min(ksg*4+4, nks)):
                            mt = t.run(lambda e, sub=sub, ksg=ksg: e.matmul(
                                psG[:, (sub-ksg*4)*128:(sub-ksg*4+1)*128],
                                Pb[:, sub*128:(sub+1)*128], bld.identb[:, :],
                                start=True, stop=True), pn,
                                ptev if ptev else prev_pt[0])
                        ptev = v.run(lambda e, ksg=ksg: e.tensor_copy(
                            PTB[:, ksg*512:(ksg+1)*512], psG[:, 0:512]), mt)
                        for sub in range(ksg*4, min(ksg*4+4, nks)):
                            mmO = t.run(lambda e, h=h, sub=sub, ksg=ksg: e.matmul(
                                psO[0:64, 0:128],
                                LKT[:, sub*128+h:sub*128+h+64],
                                PTB[:, ksg*512+(sub-ksg*4)*128:ksg*512+(sub-ksg*4+1)*128],
                                start=(sub == 0), stop=(sub == nks-1)), ptev)
                    oe = v.run(lambda e, h=h, pqt=pqt, gidx=gidx: e.tensor_copy(
                        OUTT[h:h+64, (pqt-gidx*4)*128:(pqt-gidx*4+1)*128],
                        psO[0:64, 0:128]), mmO, prev_proj[0])
                    outev.append(oe)
                    prev_pt[0] = oe
            for o in range(NCH):
                mo = t.run(lambda e, o=o: e.matmul(
                    psY[:, 0:512], WOP[:, o*128:(o+1)*128], OUTT[:, 0:512],
                    start=True, stop=True), outev, lwo, prev_proj[0])
                cp = v.run(lambda e: e.tensor_copy(E1[:, 0:512], psY[:, 0:512]),
                           mo, prev_proj[0])
                ac = gd.run(lambda e, o=o, gidx=gidx, p=p: e.dma_start(
                    out=OUT_d[o*128:(o+1)*128, gidx*512:(gidx+1)*512],
                    in_=E1[:, 0:512],
                    **({} if p == 0 else dict(accum_op=ALU.add))),
                    cp, acc.get((gidx, o)))
                acc[(gidx, o)] = ac
                prev_proj[0] = ac
        pair_last[0] = prev_proj[0]

    outtok = []
    prevld = [None, None]
    prevadd = pair_last[0]
    for c in range(NCH):
        for j in range(NT):
            b = j % 2
            buf = AF32_[:, TMP0 + b*1024: TMP0 + b*1024 + 512]
            buf2 = AF32_[:, TMP0 + b*1024 + 512: TMP0 + b*1024 + 1024]
            ld1 = q.run(lambda e, c=c, j=j, b=b: e.dma_start(
                out=buf[:, 0:512], in_=src_dram[c*128:(c+1)*128, j*512:(j+1)*512]),
                prevld[b], pair_last[0])
            ld2 = gd.run(lambda e, c=c, j=j, b=b: e.dma_start(
                out=buf2[:, 0:512], in_=OUT_d[c*128:(c+1)*128, j*512:(j+1)*512]),
                acc[(j, c)], prevld[b])
            ad = v.run(lambda e, b=b, c=c: e.tensor_tensor(
                buf[:, 0:512], buf[:, 0:512], buf2[:, 0:512], ALU.add),
                ld1, ld2, prevadd)
            ad = v.run(lambda e, b=b, c=c: e.tensor_scalar(
                buf[:, 0:512], buf[:, 0:512], BO[:, c:c+1], None, ALU.add), ad)
            prevadd = ad
            st = q.run(lambda e, c=c, j=j, b=b: e.dma_start(
                out=dst_dram[c*128:(c+1)*128, j*512:(j+1)*512], in_=buf[:, 0:512]),
                ad)
            prevld[b] = st
            outtok.append(st)
    return outtok


def build_l0a():
    nc = bass.Bass("TRN2", target_bir_lowering=False)
    io = {}
    io['x_in'] = nc.dram_tensor("x_in", [D, T], F32, kind="ExternalInput").ap()
    io['consts_in'] = nc.dram_tensor("consts_in", [128, 266], F32, kind="ExternalInput").ap()
    for nm, shape in [('l0a_wqT', [D, D]), ('l0a_wqTp', [D, D]),
                      ('l0a_wkvT', [D, D]), ('l0a_wkvTp', [D, D]),
                      ('l0a_woT', [D, D]), ('l0a_cos', [128, T]),
                      ('l0a_sins', [128, T]), ('l0a_maskb', [128, 1152]),
                      ('l0a_sel2', [128, 2]), ('l0a_repsf', [2, 128]),
                      ('l0a_lngb', [128, 2]), ('l0a_bo', [128, 8])]:
        io[nm] = nc.dram_tensor(nm, shape, F32, kind="ExternalInput").ap()
    io['y_out'] = nc.dram_tensor("y_out", [D, T], F32, kind="ExternalOutput").ap()
    es = ExitStack()
    with es:
        bld = Bld(nc, es)
        bld.init(io)
        block = es.enter_context(nc.Block())

        def prog(_):
            ct = bld.load_consts()
            ln = emit_ln_xn(bld, io['x_in'], after=ct)
            emit_l0a(bld, io['x_in'], io['y_out'], after=ln)
            bld.q.wait(bld.q.tok(), bld.gd.tok())
        block.sync(prog)
    return nc


def host_l0a_inputs(ins):
    Wq = np.asarray(ins['L0_attn_Wq'], np.float32)
    Wkv = np.asarray(ins['L0_attn_Wkv'], np.float32)
    Wo = np.asarray(ins['L0_attn_Wo'], np.float32)
    perm = np.zeros(D, np.int64)
    for r in range(D):
        h, dh = r // 64, r % 64
        perm[r] = h*64 + (dh + 32) % 64
    wqT = np.ascontiguousarray(Wq.T)
    wkvT = np.ascontiguousarray(Wkv.T)
    d = dict(
        consts_in=consts_array(),
        l0a_wqT=wqT, l0a_wqTp=np.ascontiguousarray(wqT[:, :].T[perm].T),
        l0a_wkvT=wkvT, l0a_wkvTp=np.ascontiguousarray(wkvT[:, :].T[perm].T),
        l0a_woT=np.ascontiguousarray(Wo.T),
    )
    inv_freq = 1.0 / 10000.0 ** (np.arange(0, 64, 2, dtype=np.float64) / 64)
    tpos = np.arange(T, dtype=np.float64)
    cos = np.zeros((128, T), np.float32)
    sins = np.zeros((128, T), np.float32)
    for r in range(128):
        dh = r % 64
        f = inv_freq[dh % 32]
        cos[r] = np.cos(tpos * f).astype(np.float32)
        sgn = -1.0 if dh < 32 else 1.0
        sins[r] = (sgn * np.sin(tpos * f)).astype(np.float32)
    d['l0a_cos'] = cos
    d['l0a_sins'] = sins
    maskb = np.zeros((128, 1152), np.float32)
    for pp in range(128):
        for u in range(1152):
            if u - 512 > pp:
                maskb[pp, u] = -1e30
    d['l0a_maskb'] = maskb
    sel2 = np.zeros((128, 2), np.float32)
    for r in range(128):
        sel2[r, r // 64] = 1.0
    d['l0a_sel2'] = sel2
    d['l0a_repsf'] = np.ascontiguousarray(sel2.T)
    lngb = np.zeros((128, 2), np.float32)
    lngb[:, 0] = np.tile(np.asarray(ins['L0_attn_ln_g'], np.float32), 2)
    lngb[:, 1] = np.tile(np.asarray(ins['L0_attn_ln_b'], np.float32), 2)
    d['l0a_lngb'] = lngb
    bo = np.asarray(ins['L0_attn_bo'], np.float32)
    bof = np.zeros((128, 8), np.float32)
    for c in range(NCH):
        bof[:, c] = bo[c*128:(c+1)*128]
    d['l0a_bo'] = bof
    return d



def emit_l3f(bld, src_dram, dst_dram, after=()):
    t, v, s, q, gd, g = bld.t, bld.v, bld.s, bld.q, bld.gd, bld.g
    AB, AF32_ = bld.ABF, bld.AF32
    io = bld.io
    h_d = bld.nc.dram_tensor("l3f_h_d", [D, T], BF16, kind="Internal").ap()
    bld.c['l3f_h_d'] = h_d

    WQ = AB[:, WRK:WRK+1024]
    WK = AB[:, WRK+1024:WRK+2048]
    WV = AB[:, WRK+2048:WRK+3072]
    QB = AB[:, WRK+3072:WRK+7168]
    KB = AB[:, WRK+7168:WRK+11264]
    VTM = AB[:, WRK+11264:WRK+15360]
    PTB = AB[:, WRK+15360:WRK+19456]
    OUTT = AB[:, WRK+19456:WRK+19968]
    SEL2 = AB[:, WRK+19968:WRK+19970]
    M16 = AB[0:16, WRK+19970:WRK+20994]
    SPC = AB[0:16, WRK+20994:WRK+20995]
    Pb = AB[:, SQ0:SQ0+4096]
    S_sb = AF32_[:, TMP0:TMP0+4096]
    E1 = AF32_[:, TMP1:TMP1+512]
    E2 = AF32_[:, TMP1+512:TMP1+1024]
    SS2 = AF32_[0:2, TMP1+1024:TMP1+1536]
    REPSF = AF32_[0:2, TMP1+1536:TMP1+1664]
    T16 = AF32_[0:16, TMP1+1664:TMP1+1672]
    SPT = AF32_[:, MISC+160:MISC+168]
    BQKV = AF32_[:, MISC+168:MISC+192]
    QEF = AF32_[:, MISC+192:MISC+200]
    MXC = AF32_[:, MISC+200:MISC+204]

    psA, psC, psE, psG, psS, psO, psY = (bld.pb[0], bld.pb[2], bld.pb[4],
                                         bld.pb[6], bld.pb[7], bld.pb[5],
                                         bld.pb[1])

    lw0 = [q.run(lambda e: e.dma_start(out=BQKV[:, 0:24], in_=io['l3f_bqkv'][:, :]),
                 list(after))]
    lw0.append(q.run(lambda e: e.dma_start(out=QEF[:, 0:8], in_=io['l3f_qef'][:, :])))
    lw0.append(q.run(lambda e: e.dma_start(out=T16[0:16, 0:1], in_=io['l3f_temp'][:, :])))
    lw0.append(q.run(lambda e: e.dma_start(out=REPSF[0:2, 0:128], in_=io['l3f_repsf'][:, :])))
    lw0.append(gd.run(lambda e: e.dma_start(out=SEL2[:, 0:2], in_=io['l3f_sel2'][:, :]),
                      list(after)))
    lw0.append(gd.run(lambda e: e.dma_start(out=M16[0:16, 0:1024], in_=io['l3f_m16'][:, :])))

    sp1 = s.run(lambda e: e.activation(out=T16[0:16, 1:2], in_=T16[0:16, 0:1],
                func=AF_.Abs), lw0, list(after))
    sp2 = s.run(lambda e: e.activation(out=T16[0:16, 2:3], in_=T16[0:16, 1:2],
                func=AF_.Exp, scale=-1.0), sp1)
    sp3 = s.run(lambda e: e.activation(out=T16[0:16, 3:4], in_=T16[0:16, 2:3],
                func=AF_.Ln, bias=bld.constf[0:16, 1:2]), sp2)
    sp4 = v.run(lambda e: e.tensor_scalar(T16[0:16, 4:5], T16[0:16, 0:1], 0.0,
                None, ALU.max), sp3)
    sp5 = v.run(lambda e: e.tensor_tensor(T16[0:16, 5:6], T16[0:16, 4:5],
                T16[0:16, 3:4], ALU.add), sp4)
    spc = v.run(lambda e: e.tensor_copy(SPC[0:16, 0:1], T16[0:16, 5:6]), sp5)
    mspt = None
    for p in range(8):
        mspt = t.run(lambda e, p=p: e.matmul(
            psE[:, p*2:p*2+1], M16[0:16, p*128:(p+1)*128],
            SPC[0:16, 0:1], start=True, stop=True), spc, lw0, mspt)
    spev = v.run(lambda e: e.tensor_copy(SPT[:, 0:8], psE[:, 0:16:2]), mspt)
    spev = v.run(lambda e: e.tensor_scalar(SPT[:, 0:8], SPT[:, 0:8], 0.125,
                 None, ALU.mult), spev)

    acc_h = {}
    pair_last = [spev]
    for p in range(8):
        lw = []
        for c in range(NCH):
            for wdst, wsrc in ((WQ, 'l3f_wqT'), (WK, 'l3f_wkT'), (WV, 'l3f_wvT')):
                lw.append(gd.run(lambda e, c=c, wdst=wdst, wsrc=wsrc, p=p:
                    e.dma_start(out=wdst[:, c*128:(c+1)*128],
                    in_=io[wsrc][c*128:(c+1)*128, p*128:(p+1)*128]),
                    pair_last[0] if (c == 0 and wdst is WQ) else None))
        lw += lw0

        ch = [pair_last[0]]
        tmdone = []
        for j in range(NT):
            mq = None
            for c in range(NCH):
                mq = t.run(lambda e, c=c, j=j: e.matmul(
                    psA[:, 0:512], WQ[:, c*128:(c+1)*128],
                    bld.xn(c, j*512, 512), start=(c == 0), stop=(c == NCH-1)),
                    lw, list(after), ch[0])
            e1 = v.run(lambda e, p=p: e.tensor_scalar(E1[:, 0:512], psA[:, 0:512],
                       BQKV[:, p:p+1], None, ALU.add), mq, ch[0])
            cb = v.run(lambda e: e.tensor_copy(PTB[:, 0:512], E1[:, 0:512]), e1)
            sq = s.run(lambda e: e.activation(out=PTB[:, 512:1024],
                       in_=PTB[:, 0:512], func=AF_.Square), cb)
            ms1 = t.run(lambda e: e.matmul(psC[0:2, 0:512], SEL2[:, 0:2],
                       PTB[:, 512:1024], start=True, stop=True), sq, lw, ch[0])
            r5 = s.run(lambda e: e.activation(out=SS2[0:2, 0:512],
                       in_=psC[0:2, 0:512], func=AF_.Sqrt), ms1, sq)
            r6 = v.run(lambda e: e.reciprocal(SS2[0:2, 0:512], SS2[0:2, 0:512]), r5)
            mb = t.run(lambda e: e.matmul(psE[:, 0:512], REPSF[0:2, 0:128],
                       SS2[0:2, 0:512], start=True, stop=True), r6, ch[0], spev)
            u1 = v.run(lambda e: e.tensor_tensor(E2[:, 0:512], E1[:, 0:512],
                       psE[:, 0:512], ALU.mult), mb, cb)
            u2 = v.run(lambda e, p=p: e.tensor_scalar(E2[:, 0:512], E2[:, 0:512],
                       QEF[:, p:p+1], None, ALU.add), u1)
            qb = v.run(lambda e, j=j, p=p: e.tensor_scalar(
                QB[:, j*512:(j+1)*512], E2[:, 0:512], SPT[:, p:p+1], None,
                ALU.mult), u2)
            mk = None
            for c in range(NCH):
                mk = t.run(lambda e, c=c, j=j: e.matmul(
                    psA[:, 0:512], WK[:, c*128:(c+1)*128],
                    bld.xn(c, j*512, 512), start=(c == 0), stop=(c == NCH-1)),
                    e1 if c == 0 else None)
            kb = v.run(lambda e, j=j, p=p: e.tensor_scalar(
                KB[:, j*512:(j+1)*512], psA[:, 0:512], BQKV[:, 8+p:9+p], None,
                ALU.add), mk, qb)
            mv = None
            for c in range(NCH):
                mv = t.run(lambda e, c=c, j=j: e.matmul(
                    psA[:, 0:512], WV[:, c*128:(c+1)*128],
                    bld.xn(c, j*512, 512), start=(c == 0), stop=(c == NCH-1)),
                    kb if c == 0 else None)
            vb = v.run(lambda e, p=p: e.tensor_scalar(
                PTB[:, 0:512], psA[:, 0:512], BQKV[:, 16+p:17+p], None,
                ALU.add), mv, u2, sq, ms1)
            mt = None
            for sub in range(4):
                mt = t.run(lambda e, j=j, sub=sub: e.matmul(
                    psG[:, sub*128:(sub+1)*128],
                    PTB[:, sub*128:(sub+1)*128],
                    bld.identb[:, :], start=True, stop=True), vb,
                    tmdone[-1] if tmdone else ch[0])
            tmdone.append(v.run(lambda e, j=j: e.tensor_copy(
                VTM[:, j*512:(j+1)*512], psG[:, 0:512]), mt))
            ch[0] = tmdone[-1]

        prev_pt = [tmdone[-1]]
        prev_st = [None]
        for gidx in range(8):
            outev = []
            for qt in range(gidx*4, gidx*4+4):
                for h in (0, 64):
                    sev = []
                    for kt in range(NT):
                        mmS = t.run(lambda e, h=h, qt=qt, kt=kt: e.matmul(
                            psS[:, 0:512],
                            QB[h:h+64, qt*128:(qt+1)*128],
                            KB[h:h+64, kt*512:(kt+1)*512],
                            start=True, stop=True), tmdone, prev_pt[0],
                            sev[-1] if sev else None)
                        sev.append(v.run(lambda e, kt=kt: e.tensor_copy(
                            S_sb[:, kt*512:(kt+1)*512], psS[:, 0:512]), mmS,
                            prev_pt[0]))
                    mx = v.run(lambda e: e.reduce_max(
                        MXC[:, 0:1], S_sb[:, 0:T], axis=AXX), sev)
                    nm = v.run(lambda e: e.tensor_scalar(MXC[:, 1:2], MXC[:, 0:1],
                              -1.0, None, ALU.mult), mx)
                    ex = s.run(lambda e: e.activation(
                        out=Pb[:, 0:T], in_=S_sb[:, 0:T],
                        func=AF_.Exp, bias=MXC[:, 1:2],
                        accum_out=MXC[:, 2:3]), nm, prev_pt[0])
                    rc = v.run(lambda e: e.reciprocal(MXC[:, 3:4], MXC[:, 2:3]), ex)
                    pn = v.run(lambda e: e.tensor_scalar(
                        Pb[:, 0:T], Pb[:, 0:T], MXC[:, 3:4], None, ALU.mult), rc)
                    mmO = None
                    ptev = None
                    for ksg in range(8):
                        mt = None
                        for sub in range(ksg*4, ksg*4+4):
                            mt = t.run(lambda e, sub=sub, ksg=ksg: e.matmul(
                                psG[:, (sub-ksg*4)*128:(sub-ksg*4+1)*128],
                                Pb[:, sub*128:(sub+1)*128], bld.identb[:, :],
                                start=True, stop=True), pn,
                                ptev if ptev else prev_pt[0])
                        ptev = v.run(lambda e, ksg=ksg: e.tensor_copy(
                            PTB[:, ksg*512:(ksg+1)*512], psG[:, 0:512]), mt)
                        for sub in range(ksg*4, ksg*4+4):
                            mmO = t.run(lambda e, h=h, sub=sub, ksg=ksg: e.matmul(
                                psO[0:64, 0:128],
                                VTM[:, sub*128+h:sub*128+h+64],
                                PTB[:, ksg*512+(sub-ksg*4)*128:ksg*512+(sub-ksg*4+1)*128],
                                start=(sub == 0), stop=(sub == 31)), ptev)
                    oe = v.run(lambda e, h=h, qt=qt, gidx=gidx: e.tensor_copy(
                        OUTT[h:h+64, (qt-gidx*4)*128:(qt-gidx*4+1)*128],
                        psO[0:64, 0:128]), mmO, prev_st[0])
                    outev.append(oe)
                    prev_pt[0] = oe
            st = q.run(lambda e, p=p, gidx=gidx: e.dma_start(
                out=h_d[p*128:(p+1)*128, gidx*512:(gidx+1)*512],
                in_=OUTT[:, 0:512]), outev)
            acc_h[(p, gidx)] = st
            prev_st[0] = st
        pair_last[0] = prev_st[0]

    WRS = AB[:, WRK:WRK+136]
    RSF = AF32_[0:32, TMP1+1024:TMP1+1152]
    HCH = AB[:, WRK+3072:WRK+7168]
    GA = AF32_[:, TMP1:TMP1+1024]
    MSF = AF32_[:, MISC+206:MISC+238]
    lwr = [gd.run(lambda e: e.dma_start(out=WRS[:, 0:136], in_=io['l3f_wrs'][:, :]),
                  pair_last[0])]
    BRS = AF32_[:, MISC+204:MISC+205]
    lwr.append(q.run(lambda e: e.dma_start(out=BRS[0:17, 0:1], in_=io['l3f_brs'][:, :]),
               pair_last[0]))
    prev_g = [pair_last[0]]
    for jf in range(NFOLD):
        j = jf // 4
        ldhs = []
        mm = None
        for c in range(NCH):
            ldh = gd.run(lambda e, c=c, jf=jf: e.dma_start(
                out=HCH[:, c*128:(c+1)*128],
                in_=h_d[c*128:(c+1)*128, jf*128:(jf+1)*128]),
                acc_h[(c, j)], prev_g[0])
            ldhs.append(ldh)
            mm = t.run(lambda e, c=c: e.matmul(
                psA[0:17, 0:128], WRS[:, c*17:(c+1)*17],
                HCH[:, c*128:(c+1)*128], start=(c == 0), stop=(c == NCH-1)),
                ldh, lwr, prev_g[0])
        ev = v.run(lambda e: e.tensor_scalar(RSF[0:17, 0:128], psA[0:17, 0:128],
                   BRS[0:17, 0:1], None, ALU.add), mm, prev_g[0])
        mtr = t.run(lambda e: e.matmul(psC[:, 0:17], RSF[0:17, 0:128],
                    bld.ident[0:17, 0:17], start=True, stop=True), ev, prev_g[0])
        g0 = v.run(lambda e: e.tensor_copy(GA[:, 0:17], psC[:, 0:17]), mtr)
        t1 = v.run(lambda e: e.reduce_max(GA[:, 17:18], GA[:, 0:15], axis=AXX), g0)
        t2 = v.run(lambda e: e.tensor_scalar(GA[:, 18:19], GA[:, 17:18], -1.0,
                   None, ALU.mult), t1)
        t3 = s.run(lambda e: e.activation(out=GA[:, 32:47], in_=GA[:, 0:15],
                   func=AF_.Exp, bias=GA[:, 18:19], accum_out=GA[:, 19:20]), t2)
        t4 = v.run(lambda e: e.reciprocal(GA[:, 20:21], GA[:, 19:20]), t3)
        t5 = v.run(lambda e: e.tensor_scalar(GA[:, 32:47], GA[:, 32:47],
                   GA[:, 20:21], None, ALU.mult), t4)
        m1 = v.run(lambda e: e.reduce_max(GA[:, 21:22], GA[:, 32:47], axis=AXX), t5)
        e1_ = v.run(lambda e: e.tensor_scalar(GA[:, 48:63], GA[:, 32:47],
                    GA[:, 21:22], None, ALU.is_ge), m1)
        e2_ = v.run(lambda e: e.scalar_tensor_tensor(GA[:, 64:79], GA[:, 48:63],
                    -1e30, GA[:, 32:47], ALU.mult, ALU.add), e1_)
        m2 = v.run(lambda e: e.reduce_max(GA[:, 22:23], GA[:, 64:79], axis=AXX), e2_)
        e3_ = v.run(lambda e: e.tensor_scalar(GA[:, 48:63], GA[:, 64:79],
                    GA[:, 22:23], None, ALU.is_ge), m2)
        e4_ = v.run(lambda e: e.scalar_tensor_tensor(GA[:, 64:79], GA[:, 48:63],
                    -1e30, GA[:, 64:79], ALU.mult, ALU.add), e3_)
        m3 = v.run(lambda e: e.reduce_max(GA[:, 23:24], GA[:, 64:79], axis=AXX), e4_)
        km = v.run(lambda e: e.tensor_scalar(GA[:, 48:63], GA[:, 32:47],
                   GA[:, 23:24], None, ALU.is_ge), m3)
        gm = v.run(lambda e: e.tensor_tensor(GA[:, 48:63], GA[:, 48:63],
                   GA[:, 32:47], ALU.mult), km)
        s3 = v.run(lambda e: e.tensor_reduce(GA[:, 24:25], GA[:, 48:63],
                   axis=AXX, op=ALU.add), gm)
        w1_ = v.run(lambda e: e.reduce_max(GA[:, 25:26], GA[:, 15:17], axis=AXX), g0)
        w2_ = v.run(lambda e: e.tensor_scalar(GA[:, 26:27], GA[:, 25:26], -1.0,
                    None, ALU.mult), w1_)
        w3_ = s.run(lambda e: e.activation(out=GA[:, 28:30], in_=GA[:, 15:17],
                    func=AF_.Exp, bias=GA[:, 26:27], accum_out=GA[:, 27:28]),
                    w2_, t3)
        w4_ = v.run(lambda e: e.reciprocal(GA[:, 30:31], GA[:, 27:28]), w3_)
        w5_ = v.run(lambda e: e.tensor_scalar(GA[:, 28:30], GA[:, 28:30],
                    GA[:, 30:31], None, ALU.mult), w4_)
        z1 = v.run(lambda e: e.tensor_tensor(GA[:, 31:32], GA[:, 29:30],
                   GA[:, 24:25], ALU.mult), w5_, s3)
        z2 = v.run(lambda e: e.tensor_scalar(GA[:, 31:32], GA[:, 31:32], 6.0,
                   None, ALU.mult), z1)
        z3 = v.run(lambda e: e.scalar_tensor_tensor(MSF[:, jf:jf+1], GA[:, 28:29],
                   2.0, GA[:, 31:32], ALU.mult, ALU.add), z2)
        prev_g[0] = z3

    mrow = fold_to_row(bld, MSF[:, 0:32], bld.rowTa[0:1, 0:T], bld.bounce,
                       bld.pb[3], after=[prev_g[0]], war=list(after))

    WPT = AB[:, WRK+7168:WRK+15360]
    HM = AB[:, WRK+15360:WRK+19456]
    BPC = AF32_[:, MISC+238:MISC+246]
    lwp = [gd.run(lambda e, c=c: e.dma_start(
        out=WPT[:, c*1024:(c+1)*1024], in_=io['l3f_wpT'][c*128:(c+1)*128, :]),
        prev_g[0] if c == 0 else None) for c in range(NCH)]
    lwp.append(q.run(lambda e: e.dma_start(out=BPC[:, 0:8], in_=io['l3f_bpc'][:, :]),
               prev_g[0]))
    outtok = []
    prevld = [None, None]
    prevadd = [prev_g[0]]
    prev_hm = [prev_g[0]]
    for j in range(NT):
        mbc = t.run(lambda e, j=j: e.matmul(
            bld.pb[3][:, 0:512], bld.onesrow[0:1, :],
            bld.rowTa[0:1, j*512:(j+1)*512], start=True, stop=True), mrow,
            prev_hm[0])
        hms = []
        for c in range(NCH):
            ldh = gd.run(lambda e, c=c, j=j: e.dma_start(
                out=Pb[:, c*512:(c+1)*512],
                in_=h_d[c*128:(c+1)*128, j*512:(j+1)*512]),
                acc_h[(c, j)], prev_hm[0])
            hm = v.run(lambda e, c=c: e.tensor_tensor(
                HM[:, c*512:(c+1)*512], Pb[:, c*512:(c+1)*512],
                bld.pb[3][:, 0:512], ALU.mult), ldh, mbc, prev_hm[0])
            hms.append(hm)
        prev_hm[0] = hms[-1]
        for o in range(NCH):
            mo = None
            for c in range(NCH):
                mo = t.run(lambda e, c=c, o=o: e.matmul(
                    psY[:, 0:512], WPT[:, c*1024+o*128:c*1024+(o+1)*128],
                    HM[:, c*512:(c+1)*512], start=(c == 0), stop=(c == NCH-1)),
                    lwp, hms, prevadd[0] if c == 0 else None)
            b = (j*NCH+o) % 2
            buf = AF32_[:, TMP0 + b*1024: TMP0 + b*1024 + 512]
            buf2 = AF32_[:, TMP0 + b*1024 + 512: TMP0 + b*1024 + 1024]
            ld1 = q.run(lambda e, o=o, j=j, b=b: e.dma_start(
                out=buf[:, 0:512], in_=src_dram[o*128:(o+1)*128, j*512:(j+1)*512]),
                prevld[b], prev_g[0])
            ee1 = v.run(lambda e, o=o, b=b: e.tensor_scalar(
                buf2[:, 0:512], psY[:, 0:512], BPC[:, o:o+1], None, ALU.add),
                mo, prevadd[0])
            ee2 = v.run(lambda e, b=b: e.tensor_tensor(
                buf[:, 0:512], buf[:, 0:512], buf2[:, 0:512], ALU.add), ee1, ld1)
            prevadd[0] = ee2
            st = q.run(lambda e, o=o, j=j, b=b: e.dma_start(
                out=dst_dram[o*128:(o+1)*128, j*512:(j+1)*512], in_=buf[:, 0:512]),
                ee2)
            prevld[b] = st
            outtok.append(st)
    return outtok


def build_l3f():
    nc = bass.Bass("TRN2", target_bir_lowering=False)
    io = {}
    io['x_in'] = nc.dram_tensor("x_in", [D, T], F32, kind="ExternalInput").ap()
    io['consts_in'] = nc.dram_tensor("consts_in", [128, 266], F32, kind="ExternalInput").ap()
    for nm, shape in [('l3f_wqT', [D, D]), ('l3f_wkT', [D, D]), ('l3f_wvT', [D, D]),
                      ('l3f_wpT', [D, D]), ('l3f_bqkv', [128, 24]),
                      ('l3f_qef', [128, 8]), ('l3f_temp', [16, 1]),
                      ('l3f_repsf', [2, 128]), ('l3f_sel2', [128, 2]),
                      ('l3f_m16', [16, 1024]), ('l3f_wrs', [128, 136]),
                      ('l3f_brs', [17, 1]), ('l3f_bpc', [128, 8])]:
        io[nm] = nc.dram_tensor(nm, shape, F32, kind="ExternalInput").ap()
    io['y_out'] = nc.dram_tensor("y_out", [D, T], F32, kind="ExternalOutput").ap()
    es = ExitStack()
    with es:
        bld = Bld(nc, es)
        bld.init(io)
        block = es.enter_context(nc.Block())

        def prog(_):
            ct = bld.load_consts()
            ln = emit_ln_xn(bld, io['x_in'], after=ct)
            emit_l3f(bld, io['x_in'], io['y_out'], after=ln)
            bld.q.wait(bld.q.tok(), bld.gd.tok())
        block.sync(prog)
    return nc


def host_l3f_inputs(ins):
    Wq = np.asarray(ins['L3_ffn_Wq'], np.float32)
    Wk = np.asarray(ins['L3_ffn_Wk'], np.float32)
    Wv = np.asarray(ins['L3_ffn_Wv'], np.float32)
    Wp = np.asarray(ins['L3_ffn_Wp'], np.float32)
    d = dict(
        consts_in=consts_array(),
        l3f_wqT=np.ascontiguousarray(Wq.T), l3f_wkT=np.ascontiguousarray(Wk.T),
        l3f_wvT=np.ascontiguousarray(Wv.T), l3f_wpT=np.ascontiguousarray(Wp.T),
    )
    bq = np.asarray(ins['L3_ffn_bq'], np.float32)
    bk = np.asarray(ins['L3_ffn_bk'], np.float32)
    bv = np.asarray(ins['L3_ffn_bv'], np.float32)
    bp = np.asarray(ins['L3_ffn_bp'], np.float32)
    bqkv = np.zeros((128, 24), np.float32)
    bpc = np.zeros((128, 8), np.float32)
    qef = np.zeros((128, 8), np.float32)
    qe = np.asarray(ins['L3_ffn_query_embedding'], np.float32).reshape(16, 64)
    for p in range(8):
        bqkv[:, p] = bq[p*128:(p+1)*128]
        bqkv[:, 8+p] = bk[p*128:(p+1)*128]
        bqkv[:, 16+p] = bv[p*128:(p+1)*128]
        qef[:, p] = qe[2*p:2*p+2].reshape(128)
    for c in range(NCH):
        bpc[:, c] = bp[c*128:(c+1)*128]
    d['l3f_bqkv'] = bqkv
    d['l3f_bpc'] = bpc
    d['l3f_qef'] = qef
    d['l3f_temp'] = np.asarray(ins['L3_ffn_temperature'], np.float32).reshape(16, 1)
    sel2 = np.zeros((128, 2), np.float32)
    for r in range(128):
        sel2[r, r//64] = 1.0
    d['l3f_sel2'] = sel2
    d['l3f_repsf'] = np.ascontiguousarray(sel2.T)
    m16 = np.zeros((16, 1024), np.float32)
    for p in range(8):
        for r in range(128):
            m16[2*p + r//64, p*128 + r] = 1.0
    d['l3f_m16'] = m16
    Wr = np.asarray(ins['L3_ffn_Wr'], np.float32)
    Ws = np.asarray(ins['L3_ffn_Ws'], np.float32)
    wrs = np.zeros((128, 136), np.float32)
    for c in range(NCH):
        wrs[:, c*17:c*17+15] = Wr[:, c*128:(c+1)*128].T
        wrs[:, c*17+15:c*17+17] = Ws[:, c*128:(c+1)*128].T
    d['l3f_wrs'] = wrs
    brs = np.zeros((17, 1), np.float32)
    brs[0:15, 0] = np.asarray(ins['L3_ffn_br'], np.float32)
    brs[15:17, 0] = np.asarray(ins['L3_ffn_bs'], np.float32)
    d['l3f_brs'] = brs
    return d



def emit_l2a(bld, src_dram, dst_dram, after=()):
    t, v, s, q, gd, g = bld.t, bld.v, bld.s, bld.q, bld.gd, bld.g
    AB, AF32_ = bld.ABF, bld.AF32
    io = bld.io
    nc = bld.nc
    DI = 2048

    uraw_d = nc.dram_tensor("l2a_uraw_d", [DI, T], BF16, kind="Internal").ap()
    z_d = nc.dram_tensor("l2a_z_d", [DI, T], BF16, kind="Internal").ap()
    u_d = nc.dram_tensor("l2a_u_d", [DI, T], BF16, kind="Internal").ap()
    dt_d = nc.dram_tensor("l2a_dt_d", [DI, T], BF16, kind="Internal").ap()
    y_d = nc.dram_tensor("l2a_y_d", [DI, T], BF16, kind="Internal").ap()
    lo_d = nc.dram_tensor("l2a_lo_d", [D, T], BF16, kind="Internal").ap()
    bld.c['l2a_lo_d'] = lo_d

    UB = AB[:, WRK:WRK+4128]
    BS = AB[0:16, WRK+4128:WRK+8224]
    CS = AB[0:16, WRK+8224:WRK+12320]
    LI = AB[:, WRK+4128:WRK+8224]
    YZT = AB[:, WRK+4128:WRK+12320]
    BR = AB[:, WRK+12320:WRK+16416]
    CR = AB[:, WRK+16416:WRK+20512]
    DRB = AB[0:64, WRK+12320:WRK+16416]
    HCB = AB[:, WRK+20512:WRK+22560]
    HB = AB[:, WRK+22560:WRK+24608]
    DTUH = AB[:, WRK+24608:WRK+26656]
    YSB = AB[:, WRK+24608:WRK+26656]
    DMSK = AB[:, WRK+26656:WRK+28704]
    SMSK = AB[:, WRK+28704:WRK+30752]
    REP16 = AB[0:16, WRK+30752:WRK+30880]
    HST = AB[:, WRK+30880:WRK+30896]
    WSTR = AB[:, SQ0:SQ0+2048]
    WXA = AB[:, SQ0:SQ0+1536]
    WDT = AB[0:64, SQ0+1536:SQ0+3584]
    STG = AB[:, SQ0+3584:SQ0+4096]
    UST = AB[:, WRK+20512:WRK+21024]
    DTS = AB[:, WRK+21024:WRK+21536]
    DTH = AB[:, SQ0:SQ0+2048]
    WOT = [AB[:, WRK+12320+i*1024:WRK+12320+(i+1)*1024] for i in range(14)] + \
          [AB[:, SQ0:SQ0+1024], AB[:, SQ0+1024:SQ0+2048]]

    ACOL = AF32_[:, MISC+160:MISC+416]
    CWF = AF32_[:, MISC+416:MISC+480]
    CBF = AF32_[:, MISC+480:MISC+496]
    DF = AF32_[:, MISC+496:MISC+512]
    BDT = AF32_[:, MISC+128:MISC+144]
    DAH = AF32_[:, TMP0:TMP0+2048]
    DBUH = AF32_[:, TMP0+2048:TMP0+4096]
    ACC = AF32_[:, TMP1:TMP1+T]
    E1 = AF32_[:, TMP1:TMP1+512]
    E2 = AF32_[:, TMP1+512:TMP1+1024]

    psW, psR, psY0, psY1, psY2, psY3, psS2, psX = (
        bld.pb[0], bld.pb[1], bld.pb[2], bld.pb[3], bld.pb[4], bld.pb[5],
        bld.pb[6], bld.pb[7])
    psYb = [psY0, psY1, psY2, psY3]

    lw0 = []
    lw0.append(gd.run(lambda e: e.dma_start(out=DMSK[:, :], in_=io['l2a_dmask'][:, :]),
               list(after)))
    lw0.append(gd.run(lambda e: e.dma_start(out=SMSK[:, :], in_=io['l2a_smask'][:, :])))
    lw0.append(gd.run(lambda e: e.dma_start(out=REP16[0:16, :], in_=io['l2a_rep16'][:, :])))

    final = [None]
    for L in range(2):
        dep0 = [final[0]] if final[0] else list(after)
        lwL = [q.run(lambda e, L=L: e.dma_start(out=CWF[:, 0:64],
                   in_=io['l2a_convw'][L][:, :]), dep0, lw0)]
        lwL.append(q.run(lambda e, L=L: e.dma_start(out=CBF[:, 0:16],
                   in_=io['l2a_convb'][L][:, :])))
        lwL.append(q.run(lambda e, L=L: e.dma_start(out=BDT[:, 0:16],
                   in_=io['l2a_bdt'][L][:, :]), dep0))
        lwa = q.run(lambda e, L=L: e.dma_start(out=ACOL[:, 0:256],
                   in_=io['l2a_alogf'][L][:, :]), dep0)
        a1 = s.run(lambda e: e.activation(out=ACOL[:, 0:256], in_=ACOL[:, 0:256],
                   func=AF_.Exp), lwa, dep0)
        a2 = v.run(lambda e: e.tensor_scalar(ACOL[:, 0:256], ACOL[:, 0:256],
                   -1.0, None, ALU.mult), a1, dep0)
        ldD = q.run(lambda e, L=L: e.dma_start(out=DF[:, 0:16],
                    in_=io['l2a_df'][L][:, :]), dep0)
        lwL += [a2, ldD] + lw0

        uz = {}
        li_prev = [None]
        wprev = {0: None, 1: None}
        evprev = [dep0]
        for j in range(NT):
            li = []
            if L == 1:
                for c in range(NCH):
                    li.append(gd.run(lambda e, c=c, j=j: e.dma_start(
                        out=LI[:, c*512:(c+1)*512],
                        in_=lo_d[c*128:(c+1)*128, j*512:(j+1)*512]),
                        li_prev[0], dep0))
            if L == 1:
                rhs_of = lambda c: LI[:, c*512:(c+1)*512]
            else:
                rhs_of = lambda c, j=j: bld.xn(c, j*512, 512)
            for oc in range(32):
                wb = oc % 2
                lwc = []
                for c in range(NCH):
                    lwc.append(gd.run(lambda e, c=c, oc=oc, L=L, wb=wb: e.dma_start(
                        out=WSTR[:, wb*1024 + c*128: wb*1024 + (c+1)*128],
                        in_=io['l2a_winT'][L][c*128:(c+1)*128, oc*128:(oc+1)*128]),
                        (wprev[wb] or dep0) if c == 0 else None))
                mm = None
                for c in range(NCH):
                    mm = t.run(lambda e, c=c, wb=wb: e.matmul(
                        psW[:, 0:512], WSTR[:, wb*1024 + c*128: wb*1024 + (c+1)*128],
                        rhs_of(c), start=(c == 0), stop=(c == NCH-1)),
                        lwc, li, list(after), evprev[0] if c == 0 else None)
                wprev[wb] = [mm]
                ev = v.run(lambda e: e.tensor_copy(STG[:, 0:512],
                           psW[:, 0:512]), mm, evprev[0])
                dst = uraw_d if oc < 16 else z_d
                row = oc*128 if oc < 16 else (oc-16)*128
                st = q.run(lambda e, dst=dst, row=row, j=j: e.dma_start(
                    out=dst[row:row+128, j*512:(j+1)*512], in_=STG[:, 0:512]), ev)
                uz[(oc, j)] = st
                evprev[0] = st
            li_prev[0] = evprev[0]

        conv_done = {}
        cprev = [evprev[0]]
        for ch in range(16):
            ldu = gd.run(lambda e, ch=ch: e.dma_start(
                out=UB[:, 3:3+T], in_=uraw_d[ch*128:(ch+1)*128, :]),
                [uz[(ch, jj)] for jj in range(NT)], cprev[0])
            mz = v.run(lambda e: e.memset(UB[:, 0:3], 0.0), cprev[0])
            c0 = v.run(lambda e, ch=ch: e.tensor_scalar(
                ACC[:, 0:T], UB[:, 0:T], CWF[:, ch*4:ch*4+1], None, ALU.mult),
                ldu, mz, lwL)
            for k in range(1, 4):
                c0 = v.run(lambda e, ch=ch, k=k: e.scalar_tensor_tensor(
                    ACC[:, 0:T], UB[:, k:k+T], CWF[:, ch*4+k:ch*4+k+1],
                    ACC[:, 0:T], ALU.mult, ALU.add), c0)
            si = s.run(lambda e, ch=ch: e.activation(
                out=UB[:, 3:3+T], in_=ACC[:, 0:T], func=AF_.Silu,
                bias=CBF[:, ch:ch+1]), c0)
            stu = q.run(lambda e, ch=ch: e.dma_start(
                out=u_d[ch*128:(ch+1)*128, :], in_=UB[:, 3:3+T]), si)
            conv_done[ch] = stu
            cprev[0] = stu

        lwx = []
        for c in range(16):
            lwx.append(gd.run(lambda e, c=c, L=L: e.dma_start(
                out=WXA[:, c*96:(c+1)*96], in_=io['l2a_wxT'][L][c*128:(c+1)*128, :]),
                [evprev[0]] if c == 0 else None))
        xde = [cprev[0]]
        bc_ev = []
        for j in range(NT):
            mmd = mmb = mmc = None
            for c in range(16):
                ldut = gd.run(lambda e, c=c, j=j: e.dma_start(
                    out=UST[:, 0:512],
                    in_=u_d[c*128:(c+1)*128, j*512:(j+1)*512]),
                    conv_done[c], xde[0])
                mmd = t.run(lambda e, c=c: e.matmul(
                    psY0[0:64, 0:512], WXA[:, c*96:c*96+64], UST[:, 0:512],
                    start=(c == 0), stop=(c == 15)), ldut, lwx,
                    xde[0] if c == 0 else None)
                mmb = t.run(lambda e, c=c: e.matmul(
                    psY1[0:16, 0:512], WXA[:, c*96+64:c*96+80], UST[:, 0:512],
                    start=(c == 0), stop=(c == 15)))
                mmc = t.run(lambda e, c=c: e.matmul(
                    psY2[0:16, 0:512], WXA[:, c*96+80:c*96+96], UST[:, 0:512],
                    start=(c == 0), stop=(c == 15)))
                xde[0] = mmc
            e1_ = v.run(lambda e, j=j: e.tensor_copy(DRB[0:64, j*512:(j+1)*512],
                        psY0[0:64, 0:512]), mmd)
            e2_ = v.run(lambda e, j=j: e.tensor_copy(BS[0:16, j*512:(j+1)*512],
                        psY1[0:16, 0:512]), mmb, e1_)
            e3_ = v.run(lambda e, j=j: e.tensor_copy(CS[0:16, j*512:(j+1)*512],
                        psY2[0:16, 0:512]), mmc, e2_)
            bc_ev.append(e3_)
            xde[0] = e3_

        lwd = []
        for c in range(16):
            lwd.append(gd.run(lambda e, c=c, L=L: e.dma_start(
                out=WDT[0:64, c*128:(c+1)*128],
                in_=io['l2a_wdtT'][L][:, c*128:(c+1)*128]),
                [xde[0]] if c == 0 else None))
        dt_st = {}
        dprev = [xde[0]]
        for ch in range(16):
            for j in range(NT):
                mm = t.run(lambda e, ch=ch, j=j: e.matmul(
                    psW[:, 0:512], WDT[0:64, ch*128:(ch+1)*128],
                    DRB[0:64, j*512:(j+1)*512], start=True, stop=True),
                    lwd, bc_ev, dprev[0])
                x1 = v.run(lambda e, ch=ch: e.tensor_scalar(
                    E1[:, 0:512], psW[:, 0:512], BDT[:, ch:ch+1], None,
                    ALU.add), mm, dprev[0])
                x2 = s.run(lambda e: e.activation(out=E2[:, 0:512], in_=E1[:, 0:512],
                           func=AF_.Abs), x1)
                x3 = s.run(lambda e: e.activation(out=E2[:, 0:512], in_=E2[:, 0:512],
                           func=AF_.Exp, scale=-1.0), x2)
                x4 = s.run(lambda e: e.activation(out=E2[:, 0:512], in_=E2[:, 0:512],
                           func=AF_.Ln, bias=bld.constf[:, 1:2]), x3)
                x5 = v.run(lambda e: e.tensor_scalar(E1[:, 0:512], E1[:, 0:512],
                           0.0, None, ALU.max), x1, x4)
                x6 = v.run(lambda e: e.tensor_tensor(DTS[:, 0:512], E1[:, 0:512],
                           E2[:, 0:512], ALU.add), x5)
                std = q.run(lambda e, ch=ch, j=j: e.dma_start(
                    out=dt_d[ch*128:(ch+1)*128, j*512:(j+1)*512],
                    in_=DTS[:, 0:512]), x6)
                dt_st[(ch, j)] = std
                dprev[0] = std

        rep_ev = []
        rprev = [dprev[0]]
        for j in range(NT):
            mb_ = t.run(lambda e, j=j: e.matmul(
                psW[:, 0:512], REP16[0:16, 0:128], BS[0:16, j*512:(j+1)*512],
                start=True, stop=True), lw0, bc_ev, rprev[0])
            eb_ = v.run(lambda e, j=j: e.tensor_copy(BR[:, j*512:(j+1)*512],
                        psW[:, 0:512]), mb_, rprev[0], lwd, bc_ev)
            mc_ = t.run(lambda e, j=j: e.matmul(
                psW[:, 0:512], REP16[0:16, 0:128], CS[0:16, j*512:(j+1)*512],
                start=True, stop=True), eb_)
            ec_ = v.run(lambda e, j=j: e.tensor_copy(CR[:, j*512:(j+1)*512],
                        psW[:, 0:512]), mc_)
            rep_ev.append(ec_)
            rprev[0] = ec_

        scan_prev = [rprev[0]]
        y_sts = {}
        YSF = AF32_[:, TMP1:TMP1+2048]
        for sup in range(16):
            lds = gd.run(lambda e, sup=sup: e.dma_start(
                out=UB[:, 0:T], in_=u_d[sup*128:(sup+1)*128, :]),
                conv_done[sup], scan_prev[0])
            for half in range(2):
                ldt_ = gd.run(lambda e, sup=sup, half=half: e.dma_start(
                    out=DTH[:, 0:2048],
                    in_=dt_d[sup*128:(sup+1)*128, half*2048:(half+1)*2048]),
                    [dt_st[(sup, jj)] for jj in range(half*4, half*4+4)],
                    scan_prev[0])
                dtu = v.run(lambda e, half=half: e.tensor_tensor(
                    DTUH[:, 0:2048], DTH[:, 0:2048], UB[:, half*2048:(half+1)*2048],
                    ALU.mult), ldt_, lds, scan_prev[0])
                grp_prev = [dtu]
                for gi in range(16):
                    das = []
                    for tt in range(4):
                        mmr1 = t.run(lambda e, gi=gi, tt=tt: e.matmul(
                            psR[:, 0:512], DMSK[:, gi*128:(gi+1)*128],
                            DTH[:, tt*512:(tt+1)*512], start=True, stop=True),
                            ldt_, lw0, grp_prev[0], das[-1] if das else None)
                        da = s.run(lambda e, gi=gi, tt=tt, sup=sup: e.activation(
                            out=DAH[:, tt*512:(tt+1)*512], in_=psR[:, 0:512],
                            func=AF_.Exp, scale=ACOL[:, sup*16+gi:sup*16+gi+1]),
                            mmr1, grp_prev[0])
                        mmr2 = t.run(lambda e, gi=gi, tt=tt: e.matmul(
                            psX[:, 0:512], DMSK[:, gi*128:(gi+1)*128],
                            DTUH[:, tt*512:(tt+1)*512], start=True, stop=True),
                            dtu, das[-1] if das else grp_prev[0])
                        db = v.run(lambda e, tt=tt, half=half: e.tensor_tensor(
                            DBUH[:, tt*512:(tt+1)*512], psX[:, 0:512],
                            BR[:, half*2048+tt*512:half*2048+(tt+1)*512],
                            ALU.mult), mmr2, grp_prev[0], da)
                        das.append(db)
                    if half == 0:
                        sc = v.run(lambda e: e.tensor_tensor_scan(
                            HB[:, 0:2048], DAH[:, 0:2048], DBUH[:, 0:2048], 0.0,
                            ALU.mult, ALU.add), das)
                        sv = v.run(lambda e, gi=gi: e.tensor_copy(
                            HST[:, gi:gi+1], HB[:, 2047:2048]), sc)
                    else:
                        sc = v.run(lambda e, gi=gi: e.tensor_tensor_scan(
                            HB[:, 0:2048], DAH[:, 0:2048], DBUH[:, 0:2048],
                            HST[:, gi:gi+1], ALU.mult, ALU.add), das)
                        sv = sc
                    hc = v.run(lambda e, half=half: e.tensor_tensor(
                        HCB[:, 0:2048], HB[:, 0:2048],
                        CR[:, half*2048:(half+1)*2048], ALU.mult), sv)
                    for tt in range(4):
                        grp_prev[0] = t.run(lambda e, gi=gi, tt=tt: e.matmul(
                            psYb[tt][:, 0:512], SMSK[:, gi*128:(gi+1)*128],
                            HCB[:, tt*512:(tt+1)*512], start=(gi == 0),
                            stop=(gi == 15)), hc)
                yev = None
                for tt in range(4):
                    yev = v.run(lambda e, tt=tt: e.tensor_copy(
                        YSF[:, tt*512:(tt+1)*512], psYb[tt][:, 0:512]),
                        grp_prev[0], scan_prev[0], yev)
                sty = gd.run(lambda e, sup=sup, half=half: e.dma_start(
                    out=y_d[sup*128:(sup+1)*128, half*2048:(half+1)*2048],
                    in_=YSF[:, 0:2048]), yev)
                y_sts[(sup, half)] = sty
                scan_prev[0] = sty

        lwo = []
        for c in range(16):
            lwo.append(gd.run(lambda e, c=c, L=L: e.dma_start(
                out=WOT[c][:, 0:1024], in_=io['l2a_woutT'][L][c*128:(c+1)*128, :]),
                [scan_prev[0]] if c == 0 else None))
        eprev = [scan_prev[0]]
        ldprev = [None]
        outst = []
        for j in range(NT):
            yzs = []
            for c in range(16):
                ldy = gd.run(lambda e, c=c, j=j: e.dma_start(
                    out=UB[:, 0:512], in_=y_d[c*128:(c+1)*128, j*512:(j+1)*512]),
                    [y_sts[(c, j // 4)]], eprev[0], ldprev[0])
                ldu2 = gd.run(lambda e, c=c, j=j: e.dma_start(
                    out=UB[:, 512:1024], in_=u_d[c*128:(c+1)*128, j*512:(j+1)*512]),
                    conv_done[c])
                ldz = gd.run(lambda e, c=c, j=j: e.dma_start(
                    out=UB[:, 1024:1536], in_=z_d[c*128:(c+1)*128, j*512:(j+1)*512]),
                    [uz[(16+c, j)]])
                t1 = v.run(lambda e, c=c: e.scalar_tensor_tensor(
                    E1[:, 0:512], UB[:, 512:1024], DF[:, c:c+1], UB[:, 0:512],
                    ALU.mult, ALU.add), ldy, ldu2, eprev[0])
                t2 = s.run(lambda e: e.activation(out=E2[:, 0:512],
                           in_=UB[:, 1024:1536], func=AF_.Silu), ldz, t1)
                t3 = v.run(lambda e, c=c, j=j: e.tensor_tensor(
                    YZT[:, c*512:(c+1)*512], E1[:, 0:512], E2[:, 0:512],
                    ALU.mult), t2)
                yzs.append(t3)
                eprev[0] = t3
            for o in range(NCH):
                mo = None
                for c in range(16):
                    mo = t.run(lambda e, c=c, o=o: e.matmul(
                        psX[:, 0:512], WOT[c][:, o*128:(o+1)*128],
                        YZT[:, c*512:(c+1)*512], start=(c == 0), stop=(c == 15)),
                        yzs, lwo, eprev[0] if c == 0 else None)
                if L == 0:
                    ev = v.run(lambda e: e.tensor_copy(STG[:, 0:512],
                               psX[:, 0:512]), mo, eprev[0])
                    st_ = q.run(lambda e, o=o, j=j: e.dma_start(
                        out=lo_d[o*128:(o+1)*128, j*512:(j+1)*512],
                        in_=STG[:, 0:512]), ev)
                    eprev[0] = st_
                else:
                    b = o % 2
                    buf = AF32_[:, TMP1 + b*1024: TMP1 + b*1024 + 512]
                    buf2 = AF32_[:, TMP1 + b*1024 + 512: TMP1 + b*1024 + 1024]
                    ld1 = q.run(lambda e, o=o, j=j, b=b: e.dma_start(
                        out=buf[:, 0:512],
                        in_=src_dram[o*128:(o+1)*128, j*512:(j+1)*512]),
                        ldprev[0], eprev[0])
                    ee1 = v.run(lambda e, o=o, j=j, b=b: e.tensor_tensor(
                        buf2[:, 0:512], psX[:, 0:512], bld.xn(o, j*512, 512),
                        ALU.add), mo, eprev[0])
                    ee2 = v.run(lambda e, b=b: e.tensor_tensor(
                        buf[:, 0:512], buf[:, 0:512], buf2[:, 0:512], ALU.add),
                        ee1, ld1)
                    st_ = q.run(lambda e, o=o, j=j, b=b: e.dma_start(
                        out=dst_dram[o*128:(o+1)*128, j*512:(j+1)*512],
                        in_=buf[:, 0:512]), ee2)
                    ldprev[0] = st_
                    eprev[0] = ee2
                outst.append(st_)
        final[0] = eprev[0]
    return outst


def build_l2a():
    nc = bass.Bass("TRN2", target_bir_lowering=False)
    io = {}
    io['x_in'] = nc.dram_tensor("x_in", [D, T], F32, kind="ExternalInput").ap()
    io['consts_in'] = nc.dram_tensor("consts_in", [128, 266], F32, kind="ExternalInput").ap()
    io['l2a_winT'] = [nc.dram_tensor(f"l2a_winT_{i}", [D, 4096], F32, kind="ExternalInput").ap() for i in range(2)]
    io['l2a_wxT'] = [nc.dram_tensor(f"l2a_wxT_{i}", [2048, 96], F32, kind="ExternalInput").ap() for i in range(2)]
    io['l2a_wdtT'] = [nc.dram_tensor(f"l2a_wdtT_{i}", [64, 2048], F32, kind="ExternalInput").ap() for i in range(2)]
    io['l2a_woutT'] = [nc.dram_tensor(f"l2a_woutT_{i}", [2048, D], F32, kind="ExternalInput").ap() for i in range(2)]
    io['l2a_convw'] = [nc.dram_tensor(f"l2a_convw_{i}", [128, 64], F32, kind="ExternalInput").ap() for i in range(2)]
    io['l2a_convb'] = [nc.dram_tensor(f"l2a_convb_{i}", [128, 16], F32, kind="ExternalInput").ap() for i in range(2)]
    io['l2a_bdt'] = [nc.dram_tensor(f"l2a_bdt_{i}", [128, 16], F32, kind="ExternalInput").ap() for i in range(2)]
    io['l2a_alogf'] = [nc.dram_tensor(f"l2a_alogf_{i}", [128, 256], F32, kind="ExternalInput").ap() for i in range(2)]
    io['l2a_df'] = [nc.dram_tensor(f"l2a_df_{i}", [128, 16], F32, kind="ExternalInput").ap() for i in range(2)]
    io['l2a_dmask'] = nc.dram_tensor("l2a_dmask", [128, 2048], F32, kind="ExternalInput").ap()
    io['l2a_smask'] = nc.dram_tensor("l2a_smask", [128, 2048], F32, kind="ExternalInput").ap()
    io['l2a_rep16'] = nc.dram_tensor("l2a_rep16", [16, 128], F32, kind="ExternalInput").ap()
    io['y_out'] = nc.dram_tensor("y_out", [D, T], F32, kind="ExternalOutput").ap()
    es = ExitStack()
    with es:
        bld = Bld(nc, es)
        bld.init(io)
        block = es.enter_context(nc.Block())

        def prog(_):
            ct = bld.load_consts()
            ln = emit_ln_xn(bld, io['x_in'], after=ct)
            emit_l2a(bld, io['x_in'], io['y_out'], after=ln)
            bld.q.wait(bld.q.tok(), bld.gd.tok())
        block.sync(prog)
    return nc


def host_l2a_inputs(ins):
    d = dict(consts_in=consts_array())
    for L in range(2):
        Win = np.asarray(ins['L2_attn_W_in'][L], np.float32)
        Wx = np.asarray(ins['L2_attn_W_x'][L], np.float32)
        Wdt = np.asarray(ins['L2_attn_W_dt'][L], np.float32)
        Wout = np.asarray(ins['L2_attn_W_out'][L], np.float32)
        d[f'l2a_winT_{L}'] = np.ascontiguousarray(Win.T)
        d[f'l2a_wxT_{L}'] = np.ascontiguousarray(Wx.T)
        d[f'l2a_wdtT_{L}'] = np.ascontiguousarray(Wdt.T)
        d[f'l2a_woutT_{L}'] = np.ascontiguousarray(Wout.T)
        cw = np.asarray(ins['L2_attn_conv_w'][L], np.float32)
        cb = np.asarray(ins['L2_attn_conv_b'][L], np.float32)
        bdt = np.asarray(ins['L2_attn_b_dt'][L], np.float32)
        Dv = np.asarray(ins['L2_attn_D'][L], np.float32)
        Al = np.asarray(ins['L2_attn_A_log'][L], np.float32)
        cwf = np.zeros((128, 64), np.float32)
        cbf = np.zeros((128, 16), np.float32)
        bdtf = np.zeros((128, 16), np.float32)
        dff = np.zeros((128, 16), np.float32)
        for ch in range(16):
            cwf[:, ch*4:(ch+1)*4] = cw[ch*128:(ch+1)*128, :]
            cbf[:, ch] = cb[ch*128:(ch+1)*128]
            bdtf[:, ch] = bdt[ch*128:(ch+1)*128]
            dff[:, ch] = Dv[ch*128:(ch+1)*128]
        d[f'l2a_convw_{L}'] = cwf
        d[f'l2a_convb_{L}'] = cbf
        d[f'l2a_bdt_{L}'] = bdtf
        d[f'l2a_df_{L}'] = dff
        alog = np.zeros((128, 256), np.float32)
        for gi in range(256):
            for p in range(128):
                alog[p, gi] = Al[gi*8 + p//16, p % 16]
        d[f'l2a_alogf_{L}'] = alog
    dmask = np.zeros((128, 2048), np.float32)
    smask = np.zeros((128, 2048), np.float32)
    for gi in range(16):
        for pk in range(128):
            for m in range(128):
                if pk == gi*8 + m//16:
                    dmask[pk, gi*128 + m] = 1.0
        for p in range(128):
            m2 = gi*8 + p//16
            smask[p, gi*128 + m2] = 1.0
    d['l2a_dmask'] = dmask
    d['l2a_smask'] = smask
    rep = np.zeros((16, 128), np.float32)
    for p in range(128):
        rep[p % 16, p] = 1.0
    d['l2a_rep16'] = rep
    return d



CAPG = 2560
NEXP = 16
NSLOT = NEXP * CAPG


def emit_l2_router(bld, after=(), counts_out=None):
    t, v, s, q, gd, g = bld.t, bld.v, bld.s, bld.q, bld.gd, bld.g
    AB, AF32_ = bld.ABF, bld.AF32
    io = bld.io

    GWT = AB[:, WRK:WRK+128]
    IOTB = AF32_[:, MISC+160:MISC+176]
    LG = AF32_[0:16, TMP0:TMP0+T]
    GA = AF32_[:, MISC+176:MISC+512]
    S16 = AF32_[0:16, MISC+0:MISC+128]
    psL, psT_, psB1 = bld.pb[0], bld.pb[1], bld.pb[2]

    lw = [gd.run(lambda e, c=c: e.dma_start(out=GWT[:, c*16:(c+1)*16],
          in_=io['l2f_gwT'][c*128:(c+1)*128, :]), list(after) if c == 0 else None)
          for c in range(NCH)]
    lw.append(q.run(lambda e: e.dma_start(out=IOTB[:, 0:16], in_=io['l2f_iotb'][:, :]),
              list(after)))

    lgt = []
    prev_e = [None]
    for j in range(NT):
        mm = None
        for c in range(NCH):
            mm = t.run(lambda e, c=c, j=j: e.matmul(
                psL[0:16, 0:512], GWT[:, c*16:(c+1)*16], bld.xn(c, j*512, 512),
                start=(c == 0), stop=(c == NCH-1)), lw, list(after), prev_e[0])
        ev = v.run(lambda e, j=j: e.tensor_copy(LG[:, j*512:(j+1)*512],
                   psL[0:16, 0:512]), mm, prev_e[0], list(after))
        prev_e[0] = ev
        lgt.append(ev)

    gts = []
    tprev = [lgt[-1]]
    for tt in range(NFOLD):
        mtr = t.run(lambda e, tt=tt: e.matmul(
            psT_[:, 0:16], LG[:, tt*128:(tt+1)*128], bld.ident[0:16, 0:16],
            start=True, stop=True), lgt[tt//4], tprev[0])
        g0 = v.run(lambda e: e.tensor_copy(GA[:, 192:208], psT_[:, 0:16]), mtr,
                   tprev[0])
        t1 = v.run(lambda e, tt=tt: e.reduce_max(GA[:, tt:tt+1], GA[:, 192:208],
                   axis=AXX), g0)
        t2 = v.run(lambda e, tt=tt: e.tensor_scalar(GA[:, 32+tt:33+tt],
                   GA[:, tt:tt+1], -1.0, None, ALU.mult), t1)
        t3 = s.run(lambda e, tt=tt: e.activation(out=GA[:, 208:224],
                   in_=GA[:, 192:208], func=AF_.Exp, bias=GA[:, 32+tt:33+tt],
                   accum_out=GA[:, 224:225]), t2)
        t4 = v.run(lambda e, tt=tt: e.reciprocal(GA[:, 64+tt:65+tt],
                   GA[:, 224:225]), t3)
        t5 = v.run(lambda e, tt=tt: e.tensor_scalar(GA[:, 208:224],
                   GA[:, 192:208], GA[:, tt:tt+1], None, ALU.is_ge), t4)
        t6 = v.run(lambda e: e.tensor_tensor(GA[:, 208:224], GA[:, 208:224],
                   IOTB[:, 0:16], ALU.mult), t5, lw)
        t7 = v.run(lambda e, tt=tt: e.tensor_reduce(GA[:, 96+tt:97+tt],
                   GA[:, 208:224], axis=AXX, op=ALU.add), t6)
        gts.append(t7)
        tprev[0] = t7

    v1row = fold_to_row(bld, GA[:, 0:32], bld.rowTa[0:1, 0:T], bld.bounce,
                        bld.pb[3], after=[gts[-1]], war=list(after))
    OH = AF32_[0:16, TMP1:TMP1+T]
    RK = AF32_[0:16, TMP0:TMP0+T]
    ohs = []
    ohprev = [gts[-1]]
    for j in range(NT):
        mbc = t.run(lambda e, j=j: e.matmul(
            psB1[0:16, 0:512], bld.onesrow[0:1, 0:16],
            bld.rowTa[0:1, j*512:(j+1)*512], start=True, stop=True), v1row,
            ohprev[0])
        oh = v.run(lambda e, j=j: e.tensor_tensor(
            OH[:, j*512:(j+1)*512], LG[:, j*512:(j+1)*512], psB1[0:16, 0:512],
            ALU.is_ge), mbc, ohprev[0], lgt)
        ohs.append(oh)
        ohprev[0] = oh
    rk = v.run(lambda e: e.tensor_tensor_scan(
        RK[:, 0:T], OH[:, 0:T], OH[:, 0:T], 0.0, ALU.add, ALU.bypass),
        ohs, lgt, gts)
    res = dict(OH=OH, RK=RK, GA=GA, S16=S16, rk_tok=rk, oh_toks=ohs, gts=gts)
    if counts_out is not None:
        cp = v.run(lambda e: e.tensor_copy(GA[0:16, 225:226], RK[0:16, T-1:T]), rk)
        stc = q.run(lambda e: e.dma_start(out=counts_out[:, :],
                    in_=GA[0:16, 225:226]), cp)
        res['counts_st'] = stc
    return res


def emit_l2f(bld, src_dram, dst_dram, after=()):
    t, v, s, q, gd, g = bld.t, bld.v, bld.s, bld.q, bld.gd, bld.g
    AB, AF32_ = bld.ABF, bld.AF32
    io = bld.io
    nc = bld.nc
    C1 = 0.7978845608028654

    disp_d = nc.dram_tensor("l2f_disp_d", [NSLOT, D], BF16, kind="Internal").ap()
    eo_d = nc.dram_tensor("l2f_eo_d", [NSLOT, D], BF16, kind="Internal").ap()

    r = emit_l2_router(bld, after=after)
    OH, RK, GA, S16 = r['OH'], r['RK'], r['GA'], r['S16']
    rk = r['rk_tok']
    bnd_reg = bld.g.obj.to_reg(NSLOT - 1)

    CA = AF32_[0:8, MISC+128:MISC+144]
    RM = AF32_[0:8, MISC+144:MISC+145]
    PFX = AF32_[0:16, MISC+145:MISC+146]
    THR = AF32_[0:16, MISC+146:MISC+147]
    lca = q.run(lambda e: e.dma_start(out=CA[0:8, 0:16], in_=io['l2f_counts'][:, :]),
                list(after))
    lrm = q.run(lambda e: e.dma_start(out=RM[0:8, 0:1], in_=io['l2f_rmask'][:, :]),
                list(after))
    mpfx = t.run(lambda e: e.matmul(bld.pb[3][0:16, 0:1], CA[0:8, 0:16],
                 RM[0:8, 0:1], start=True, stop=True), lca, lrm, r['gts'])
    epfx = v.run(lambda e: e.tensor_copy(PFX[0:16, 0:1], bld.pb[3][0:16, 0:1]),
                 mpfx, rk)
    ethr = v.run(lambda e: e.tensor_scalar(THR[0:16, 0:1], PFX[0:16, 0:1],
                 -1.0, float(CAPG + 1), ALU.mult, ALU.add), epfx)

    fprev = [ethr]
    for tt in range(NFOLD):
        p1 = v.run(lambda e, tt=tt: e.tensor_tensor(
            S16[0:16, 0:128], OH[:, tt*128:(tt+1)*128],
            RK[:, tt*128:(tt+1)*128], ALU.mult), rk, fprev[0])
        m1 = t.run(lambda e, tt=tt: e.matmul(
            bld.pb[4][:, tt:tt+1], S16[0:16, 0:128], bld.constf[0:16, 1:2],
            start=True, stop=True), p1)
        p2 = v.run(lambda e, tt=tt: e.tensor_scalar(
            S16[0:16, 0:128], RK[:, tt*128:(tt+1)*128], THR[0:16, 0:1], None,
            ALU.is_lt), m1, ethr)
        p3 = v.run(lambda e, tt=tt: e.tensor_tensor(
            S16[0:16, 0:128], S16[0:16, 0:128], OH[:, tt*128:(tt+1)*128],
            ALU.mult), p2)
        m2 = t.run(lambda e, tt=tt: e.matmul(
            bld.pb[5][:, tt:tt+1], S16[0:16, 0:128], bld.constf[0:16, 1:2],
            start=True, stop=True), p3)
        fprev[0] = m2
    ev1 = v.run(lambda e: e.tensor_copy(GA[:, 128:160], bld.pb[4][:, 0:32]),
                fprev[0])
    ev2 = v.run(lambda e: e.tensor_copy(GA[:, 160:192], bld.pb[5][:, 0:32]), ev1)
    s1 = v.run(lambda e: e.tensor_scalar(GA[:, 128:160], GA[:, 128:160], -1.0,
               None, ALU.add), ev2)
    s2 = v.run(lambda e: e.tensor_scalar(GA[:, 192:224], GA[:, 96:128],
               float(CAPG), None, ALU.mult), s1, r['gts'])
    s3 = v.run(lambda e: e.tensor_tensor(GA[:, 192:224], GA[:, 192:224],
               GA[:, 128:160], ALU.add), s2)
    s4 = v.run(lambda e: e.tensor_tensor(GA[:, 192:224], GA[:, 192:224],
               GA[:, 160:192], ALU.mult), s3)
    s5 = v.run(lambda e: e.tensor_scalar(GA[:, 224:256], GA[:, 160:192],
               -50000.0, 50000.0, ALU.mult, ALU.add), s4)
    s6 = v.run(lambda e: e.tensor_tensor(GA[:, 192:224], GA[:, 192:224],
               GA[:, 224:256], ALU.add), s5)
    si = v.run(lambda e: e.tensor_copy(bld.AI32b[:, 0:32], GA[:, 192:224]), s6)

    HHz = AB[:, WRK+4096:WRK+20480]
    mz = v.run(lambda e: e.memset(HHz[:, 0:1024], 0.0), list(after))
    zsts = []
    zprev = None
    for k in range(NSLOT // 128):
        zprev = gd.run(lambda e, k=k: e.dma_start(
            out=disp_d[k*128:(k+1)*128, :], in_=HHz[:, 0:1024]), mz)
        zsts.append(zprev)
    XTS = AB[:, WRK+24576:WRK+25600]
    disp_sts = []
    dprev = [si]
    for tt in range(NFOLD):
        mtq = None
        for c in range(NCH):
            bank = bld.pb[6] if c < 4 else bld.pb[7]
            mtq = t.run(lambda e, c=c, tt=tt: e.matmul(
                bank[:, (c % 4)*128:(c % 4 + 1)*128],
                bld.xn(c, tt*128, 128), bld.identb[:, :],
                start=True, stop=True), dprev[0], r['oh_toks'])
        e1 = v.run(lambda e: e.tensor_copy(XTS[:, 0:512], bld.pb[6][:, 0:512]),
                   mtq, dprev[0])
        e2 = v.run(lambda e: e.tensor_copy(XTS[:, 512:1024], bld.pb[7][:, 0:512]),
                   e1)
        sc = gd.run(lambda e, tt=tt: e.indirect_dma_start(
            out=disp_d[:, :],
            out_offset=IndirectOffsetOnAxis(bld.AI32b[:, tt:tt+1], 0),
            in_=XTS[:, 0:1024], in_offset=None,
            bounds_check=bnd_reg, oob_is_err=False), e2, si, zsts)
        disp_sts.append(sc)
        dprev[0] = sc

    W1 = [AB[:, c*4096:(c+1)*4096] for c in range(NCH)]
    DSP = AB[:, WRK:WRK+4096]
    HH = AB[:, WRK+4096:WRK+20480]
    W2S = AB[:, WRK+20480:WRK+22528]
    EOT = AB[:, SQ0:SQ0+1024]
    EFM = DSP
    E1s = AF32_[:, TMP0:TMP0+512]
    E2s = AF32_[:, TMP0+512:TMP0+1024]
    e_prev = [disp_sts[-1]]
    w2prev = {0: None, 1: None}
    eo_sts = {}
    for ei in range(NEXP):
        lw1 = []
        for c in range(NCH):
            lw1.append(gd.run(lambda e, c=c, ei=ei: e.dma_start(
                out=W1[c][:, 0:4096], in_=io['l2f_w1'][ei][c*128:(c+1)*128, :]),
                (e_prev[0], disp_sts) if c == 0 else None))
        for st in range(CAPG // 512):
            for kk in range(4):
                ld = gd.run(lambda e, ei=ei, st=st, kk=kk: e.dma_start(
                    out=EOT[:, 0:1024],
                    in_=disp_d[ei*CAPG + st*512 + kk*128: ei*CAPG + st*512 + (kk+1)*128, :]),
                    disp_sts, e_prev[0])
                mt = None
                for c in range(NCH):
                    bank = bld.pb[6] if c < 4 else bld.pb[7]
                    mt = t.run(lambda e, c=c: e.matmul(
                        bank[:, (c % 4)*128:(c % 4+1)*128],
                        EOT[:, c*128:(c+1)*128], bld.identb[:, :],
                        start=True, stop=True), ld, e_prev[0])
                evl = None
                for c in range(NCH):
                    bank = bld.pb[6] if c < 4 else bld.pb[7]
                    evl = v.run(lambda e, c=c, kk=kk: e.tensor_copy(
                        DSP[:, c*512 + kk*128: c*512 + (kk+1)*128],
                        bank[:, (c % 4)*128:(c % 4+1)*128]), mt, e_prev[0])
                e_prev[0] = evl
            hprev = None
            for fc in range(32):
                mmh = None
                for c in range(NCH):
                    mmh = t.run(lambda e, c=c, fc=fc: e.matmul(
                        bld.pb[0][:, 0:512], W1[c][:, fc*128:(fc+1)*128],
                        DSP[:, c*512:(c+1)*512], start=(c == 0),
                        stop=(c == NCH-1)), lw1, e_prev[0],
                        hprev if c == 0 else None)
                g1 = s.run(lambda e: e.activation(out=E1s[:, 0:512],
                           in_=bld.pb[0][:, 0:512], func=AF_.Square), mmh, hprev)
                g2 = v.run(lambda e: e.tensor_scalar(
                    E1s[:, 0:512], E1s[:, 0:512], 0.044715, 1.0,
                    ALU.mult, ALU.add), g1, hprev)
                g3 = v.run(lambda e: e.tensor_tensor(
                    E2s[:, 0:512], E1s[:, 0:512], bld.pb[0][:, 0:512], ALU.mult),
                    g2)
                g4 = s.run(lambda e: e.activation(out=E1s[:, 0:512],
                           in_=E2s[:, 0:512], func=AF_.Sigmoid,
                           scale=2.0*C1), g3)
                g5 = v.run(lambda e, fc=fc: e.tensor_tensor(
                    HH[:, fc*512:(fc+1)*512], E1s[:, 0:512],
                    bld.pb[0][:, 0:512], ALU.mult), g4)
                hprev = g5
            evs = [hprev]
            for fc in range(32):
                wb = fc % 2
                lw2 = gd.run(lambda e, ei=ei, fc=fc, wb=wb: e.dma_start(
                    out=W2S[:, wb*1024:(wb+1)*1024],
                    in_=io['l2f_w2'][ei][fc*128:(fc+1)*128, :]),
                    (w2prev[wb], hprev) if True else None)
                mmo = None
                for o in range(NCH):
                    mmo = t.run(lambda e, fc=fc, o=o, wb=wb: e.matmul(
                        bld.pb[o][:, 0:512],
                        W2S[:, wb*1024 + o*128: wb*1024 + (o+1)*128],
                        HH[:, fc*512:(fc+1)*512], start=(fc == 0),
                        stop=(fc == 31)), lw2, hprev,
                        evs if fc == 0 and o == 0 else None)
                w2prev[wb] = mmo
            evs = []
            for o in range(NCH):
                evs.append(v.run(lambda e, o=o: e.tensor_copy(
                    EFM[:, o*512:(o+1)*512], bld.pb[o][:, 0:512]), w2prev[1],
                    w2prev[0]))
            tprev2 = [evs[-1]]
            for kk in range(4):
                mt = None
                for c in range(NCH):
                    bank = bld.pb[6] if c < 4 else bld.pb[7]
                    mt = t.run(lambda e, c=c, kk=kk: e.matmul(
                        bank[:, (c % 4)*128:(c % 4+1)*128],
                        EFM[:, c*512 + kk*128: c*512 + (kk+1)*128],
                        bld.identb[:, :], start=True, stop=True), evs,
                        tprev2[0])
                ev1_ = v.run(lambda e: e.tensor_copy(
                    EOT[:, 0:512], bld.pb[6][:, 0:512]), mt, tprev2[0])
                ev2_ = v.run(lambda e: e.tensor_copy(
                    EOT[:, 512:1024], bld.pb[7][:, 0:512]), ev1_)
                ste = q.run(lambda e, ei=ei, st=st, kk=kk: e.dma_start(
                    out=eo_d[ei*CAPG + st*512 + kk*128: ei*CAPG + st*512 + (kk+1)*128, :],
                    in_=EOT[:, 0:1024]), ev2_)
                eo_sts[(ei, st, kk)] = ste
                tprev2[0] = ste
            e_prev[0] = tprev2[0]

    GT = AB[:, WRK+20480:WRK+21504]
    GTS = AB[:, WRK+21504:WRK+22528]
    all_eo = [eo_sts[(ei, st, kk)] for ei in range(NEXP)
              for st in range(CAPG // 512) for kk in range(4)]
    cprev = [e_prev[0]]
    ldx_prev = [None]
    outtok = []
    for j in range(NT):
        gats = []
        for k in range(4):
            tt = j*4 + k
            mz2 = v.run(lambda e: e.memset(GT[:, 0:1024], 0.0), cprev[0])
            ga = gd.run(lambda e, tt=tt: e.indirect_dma_start(
                out=GT[:, 0:1024], out_offset=None,
                in_=eo_d[:, :],
                in_offset=IndirectOffsetOnAxis(bld.AI32b[:, tt:tt+1], 0),
                bounds_check=bnd_reg, oob_is_err=False), mz2, all_eo, si)
            ps_ = v.run(lambda e, tt=tt: e.tensor_scalar(
                GTS[:, 0:1024], GT[:, 0:1024], GA[:, 64+tt:65+tt], None,
                ALU.mult), ga, cprev[0], gats[-1] if gats else None)
            for c in range(NCH):
                gats.append(t.run(lambda e, c=c, k=k: e.matmul(
                    bld.pb[c][:, k*128:(k+1)*128], GTS[:, c*128:(c+1)*128],
                    bld.identb[:, :], start=True, stop=True), ps_,
                    cprev[0] if (k == 0 and c == 0) else None))
            cprev[0] = ps_
        for c in range(NCH):
            b = c % 2
            buf = AF32_[:, TMP0 + b*1024: TMP0 + b*1024 + 512]
            buf2 = AF32_[:, TMP0 + b*1024 + 512: TMP0 + b*1024 + 1024]
            ld1 = q.run(lambda e, c=c, j=j, b=b: e.dma_start(
                out=buf[:, 0:512], in_=src_dram[c*128:(c+1)*128, j*512:(j+1)*512]),
                ldx_prev[0], gats)
            ee1 = v.run(lambda e, c=c, b=b: e.tensor_tensor(
                buf2[:, 0:512], bld.pb[c][:, 0:512], buf[:, 0:512], ALU.add),
                gats, ld1)
            st_ = q.run(lambda e, c=c, j=j, b=b: e.dma_start(
                out=dst_dram[c*128:(c+1)*128, j*512:(j+1)*512], in_=buf2[:, 0:512]),
                ee1)
            ldx_prev[0] = st_
            outtok.append(st_)
        cprev[0] = outtok[-1]
    return outtok


def build_l2r():
    nc = bass.Bass("TRN2", target_bir_lowering=False)
    io = {}
    io['x_in'] = nc.dram_tensor("x_in", [D, T], F32, kind="ExternalInput").ap()
    io['consts_in'] = nc.dram_tensor("consts_in", [128, 266], F32, kind="ExternalInput").ap()
    io['l2f_gwT'] = nc.dram_tensor("l2f_gwT", [D, 16], F32, kind="ExternalInput").ap()
    io['l2f_iotb'] = nc.dram_tensor("l2f_iotb", [128, 16], F32, kind="ExternalInput").ap()
    io['counts_out'] = nc.dram_tensor("counts_out", [16, 1], F32, kind="ExternalOutput").ap()
    es = ExitStack()
    with es:
        bld = Bld(nc, es)
        bld.init(io)
        block = es.enter_context(nc.Block())

        def prog(_):
            ct = bld.load_consts()
            ln = emit_ln_xn(bld, io['x_in'], after=ct)
            emit_l2_router(bld, after=ln, counts_out=io['counts_out'])
            bld.q.wait(bld.q.tok(), bld.gd.tok())
        block.sync(prog)
    return nc


def build_l2f():
    nc = bass.Bass("TRN2", target_bir_lowering=False)
    io = {}
    io['x_in'] = nc.dram_tensor("x_in", [D, T], F32, kind="ExternalInput").ap()
    io['consts_in'] = nc.dram_tensor("consts_in", [128, 266], F32, kind="ExternalInput").ap()
    io['l2f_gwT'] = nc.dram_tensor("l2f_gwT", [D, 16], F32, kind="ExternalInput").ap()
    io['l2f_iotb'] = nc.dram_tensor("l2f_iotb", [128, 16], F32, kind="ExternalInput").ap()
    io['l2f_counts'] = nc.dram_tensor("l2f_counts", [8, 16], F32, kind="ExternalInput").ap()
    io['l2f_rmask'] = nc.dram_tensor("l2f_rmask", [8, 1], F32, kind="ExternalInput").ap()
    io['l2f_w1'] = [nc.dram_tensor(f"l2f_w1_{i}", [D, 4096], F32, kind="ExternalInput").ap() for i in range(16)]
    io['l2f_w2'] = [nc.dram_tensor(f"l2f_w2_{i}", [4096, D], F32, kind="ExternalInput").ap() for i in range(16)]
    io['y_out'] = nc.dram_tensor("y_out", [D, T], F32, kind="ExternalOutput").ap()
    es = ExitStack()
    with es:
        bld = Bld(nc, es)
        bld.init(io)
        block = es.enter_context(nc.Block())

        def prog(_):
            ct = bld.load_consts()
            ln = emit_ln_xn(bld, io['x_in'], after=ct)
            emit_l2f(bld, io['x_in'], io['y_out'], after=ln)
            bld.q.wait(bld.q.tok(), bld.gd.tok())
        block.sync(prog)
    return nc


def host_l2f_inputs(ins):
    gw = np.asarray(ins['L2_ffn_gate_w'], np.float32)
    d = dict(
        consts_in=consts_array(),
        l2f_gwT=np.ascontiguousarray(gw.T),
        l2f_iotb=np.tile(np.arange(16, dtype=np.float32)[None, :], (128, 1)),
    )
    w1 = np.asarray(ins['L2_ffn_w1'], np.float32)
    w2 = np.asarray(ins['L2_ffn_w2'], np.float32)
    for i in range(16):
        d[f'l2f_w1_{i}'] = np.ascontiguousarray(w1[i])
        d[f'l2f_w2_{i}'] = np.ascontiguousarray(w2[i])
    return d



def build_l1a():
    nc = bass.Bass("TRN2", target_bir_lowering=False)
    io = {}
    io['x_in'] = nc.dram_tensor("x_in", [D, T], F32, kind="ExternalInput").ap()
    io['consts_in'] = nc.dram_tensor("consts_in", [128, 266], F32, kind="ExternalInput").ap()
    io['l1a_WaT'] = nc.dram_tensor("l1a_WaT", [D, 32], F32, kind="ExternalInput").ap()
    io['l1a_WinT'] = nc.dram_tensor("l1a_WinT", [D, 32], F32, kind="ExternalInput").ap()
    io['l1a_WoutT'] = nc.dram_tensor("l1a_WoutT", [32, D], F32, kind="ExternalInput").ap()
    io['l1a_bias'] = nc.dram_tensor("l1a_bias", [128, 26], F32, kind="ExternalInput").ap()
    io['y_out'] = nc.dram_tensor("y_out", [D, T], F32, kind="ExternalOutput").ap()
    es = ExitStack()
    with es:
        bld = Bld(nc, es)
        bld.init(io)
        block = es.enter_context(nc.Block())

        def prog(_):
            ct = bld.load_consts()
            ln = emit_ln_xn(bld, io['x_in'], after=ct)
            emit_l1_attn(bld, io['x_in'], io['y_out'], after=ln)
            bld.q.wait(bld.q.tok(), bld.gd.tok())
        block.sync(prog)
    return nc


def host_l1a_inputs(ins):
    bias = np.zeros((128, 26), np.float32)
    bias[0:32, 0] = ins['L1_attn_b_a'].astype(np.float32)
    bias[0:32, 1] = ins['L1_attn_b_in'].astype(np.float32)
    bo = ins['L1_attn_b_out'].astype(np.float32)
    ga = ins['L1_attn_ln_gamma'].astype(np.float32)
    be = ins['L1_attn_ln_beta'].astype(np.float32)
    for c in range(NCH):
        bias[:, 2+c] = bo[c*128:(c+1)*128]
        bias[:, 10+c] = ga[c*128:(c+1)*128]
        bias[:, 18+c] = be[c*128:(c+1)*128]
    return dict(
        consts_in=consts_array(),
        l1a_WaT=np.ascontiguousarray(np.asarray(ins['L1_attn_W_a']).T.astype(np.float32)),
        l1a_WinT=np.ascontiguousarray(np.asarray(ins['L1_attn_W_in']).T.astype(np.float32)),
        l1a_WoutT=np.ascontiguousarray(np.asarray(ins['L1_attn_W_out']).T.astype(np.float32)),
        l1a_bias=bias,
    )


def build_l0f():
    nc = bass.Bass("TRN2", target_bir_lowering=False)
    io = {}
    io['x_in'] = nc.dram_tensor("x_in", [D, T], F32, kind="ExternalInput").ap()
    io['consts_in'] = nc.dram_tensor("consts_in", [128, 266], F32, kind="ExternalInput").ap()
    io['l0f_rw'] = nc.dram_tensor("l0f_rw", [D, 8], F32, kind="ExternalInput").ap()
    io['l0f_rb'] = nc.dram_tensor("l0f_rb", [8, 1], F32, kind="ExternalInput").ap()
    io['l0f_w1'] = [nc.dram_tensor(f"l0f_w1_{i}", [D, 512], F32, kind="ExternalInput").ap() for i in range(10)]
    io['l0f_w3'] = [nc.dram_tensor(f"l0f_w3_{i}", [D, 512], F32, kind="ExternalInput").ap() for i in range(10)]
    io['l0f_w2'] = [nc.dram_tensor(f"l0f_w2_{i}", [512, D], F32, kind="ExternalInput").ap() for i in range(10)]
    io['y_out'] = nc.dram_tensor("y_out", [D, T], F32, kind="ExternalOutput").ap()
    es = ExitStack()
    with es:
        bld = Bld(nc, es)
        bld.init(io)
        block = es.enter_context(nc.Block())

        def prog(_):
            ct = bld.load_consts()
            ln = emit_ln_xn(bld, io['x_in'], after=ct)
            emit_l0_ffn(bld, io['x_in'], io['y_out'], after=ln)
            bld.q.wait(bld.q.tok(), bld.gd.tok())
        block.sync(prog)
    return nc


def host_l0f_inputs(ins):
    w1 = np.concatenate([np.asarray(ins['L0_ffn_routed_w1']),
                         np.asarray(ins['L0_ffn_shared_w1'])], axis=0).astype(np.float32)
    w3 = np.concatenate([np.asarray(ins['L0_ffn_routed_w3']),
                         np.asarray(ins['L0_ffn_shared_w3'])], axis=0).astype(np.float32)
    w2 = np.concatenate([np.asarray(ins['L0_ffn_routed_w2']),
                         np.asarray(ins['L0_ffn_shared_w2'])], axis=0).astype(np.float32)
    d = dict(
        consts_in=consts_array(),
        l0f_rw=np.ascontiguousarray(np.asarray(ins['L0_ffn_router_w'], np.float32)),
        l0f_rb=np.asarray(ins['L0_ffn_router_b'], np.float32).reshape(8, 1),
    )
    for i in range(10):
        d[f'l0f_w1_{i}'] = np.ascontiguousarray(w1[i])
        d[f'l0f_w3_{i}'] = np.ascontiguousarray(w3[i])
        d[f'l0f_w2_{i}'] = np.ascontiguousarray(w2[i])
    return d


def kernel(**inputs):
    ins = inputs
    x = np.asarray(ins['L0_attn_x'])
    cur = [np.ascontiguousarray(x[c].T.astype(np.float32)) for c in range(8)]

    def run(build_fn, host_fn, extra_per_core=None):
        nonlocal cur
        nc = build_fn()
        im = host_fn(ins)
        ims = []
        for c in range(8):
            d = dict(im)
            d['x_in'] = cur[c]
            if extra_per_core:
                d.update(extra_per_core(c))
            ims.append(d)
        res = run_bass_kernel_spmd(nc, ims, core_ids=list(range(8)))
        cur = [np.ascontiguousarray(np.asarray(res.results[c]['y_out']))
               for c in range(8)]
        return res

    run(build_l0a, host_l0a_inputs)
    run(build_l0f, host_l0f_inputs)
    run(build_l1a, host_l1a_inputs)
    run(build_l1f, host_l1f_inputs)
    run(build_l2a, host_l2a_inputs)

    ncr = build_l2r()
    imf = host_l2f_inputs(ins)
    imsr = []
    for c in range(8):
        d = dict(consts_in=imf['consts_in'], l2f_gwT=imf['l2f_gwT'],
                 l2f_iotb=imf['l2f_iotb'], x_in=cur[c])
        imsr.append(d)
    resr = run_bass_kernel_spmd(ncr, imsr, core_ids=list(range(8)))
    counts_all = np.stack([np.asarray(resr.results[c]['counts_out'])[:, 0]
                           for c in range(8)], 0).astype(np.float32)
    counts_all = np.ascontiguousarray(counts_all)

    def l2f_extra(c):
        rm = np.zeros((8, 1), np.float32)
        rm[:c, 0] = 1.0
        return dict(l2f_counts=counts_all, l2f_rmask=rm)

    run(build_l2f, host_l2f_inputs, l2f_extra)
    run(build_l3a, host_l3a_inputs)
    run(build_l3f, host_l3f_inputs)

    out = np.stack([cur[c].T for c in range(8)], 0)
    return np.ascontiguousarray(out.astype(np.float32))
